# version 02
# baseline (speedup 1.0000x reference)
import contextlib
import numpy as np
import ml_dtypes
import concourse.bass as bass
import concourse.mybir as mybir
from concourse.bass_utils import run_bass_kernel_spmd

F32 = mybir.dt.float32
BF16 = mybir.dt.bfloat16
I32 = mybir.dt.int32
AF = mybir.ActivationFunctionType
OP = mybir.AluOpType
AX = mybir.AxisListType

S = 2048
D = 1024
H = 16
HD = 64
NF = 2816
NT = S // 128
NCh = D // 128
NNC = NF // 128
HALF = 1024
NHF = NF // 2
QW = NHF
EPS = 1e-8
bf16np = ml_dtypes.bfloat16


class Eng:
    def __init__(self, nc, eng, sem, dma_sems=None):
        self.nc, self.eng, self.sem = nc, eng, sem
        self.n = 0
        self.dma_sems = dma_sems or []
        self.dma_cnt = [0] * len(self.dma_sems)
        self.dma_i = 0
        self.watermark = {}

    def wait_tok(self, tok):
        sem, val = tok
        key = id(sem)
        if self.watermark.get(key, -1) >= val:
            return
        self.eng.wait_ge(sem, val)
        self.watermark[key] = val


class Buf:
    __slots__ = ("wtok", "rtoks")
    def __init__(self):
        self.wtok = None
        self.rtoks = []


class Tracker:
    def __init__(self):
        self.bufs = {}
        self.drainers = {}

    def add_drainer(self, eng, junk_fn):
        self.drainers[id(eng.sem)] = [eng, junk_fn, -1, -1]

    def launder(self, consumer, tok):
        sem, val = tok
        if sem is consumer.sem:
            return tok
        d = self.drainers.get(id(sem))
        if d is None:
            return tok
        eng, junk_fn, covered, junk_val = d
        if covered >= val:
            return (sem, junk_val)
        pre = eng.n
        n0 = eng.n
        junk_fn()
        junk_fn()
        d[2] = pre
        d[3] = eng.n
        assert eng.n > n0
        return (sem, eng.n)

    def buf(self, h):
        k = id(h)
        if k not in self.bufs:
            self.bufs[k] = Buf()
        return self.bufs[k]

    def set_mutex(self, *pairs):
        self.mutex = pairs

    def op(self, eng, build, reads=(), writes=(), dma=False):
        for a, b in getattr(self, "mutex", ()):
            if eng is a and b.n:
                eng.wait_tok((b.sem, b.n))
            elif eng is b and a.n:
                eng.wait_tok((a.sem, a.n))
        toks = []
        for h in reads:
            b = self.buf(h)
            if b.wtok is not None:
                toks.append(b.wtok)
        for h in writes:
            b = self.buf(h)
            if b.wtok is not None:
                toks.append(b.wtok)
            toks.extend(b.rtoks)
        for t in toks:
            eng.wait_tok(self.launder(eng, t))
        if dma:
            s_i = eng.dma_i % len(eng.dma_sems)
            eng.dma_i += 1
            if eng.dma_cnt[s_i]:
                eng.eng.wait_ge(eng.dma_sems[s_i], eng.dma_cnt[s_i])
            instr = build()
            eng.dma_cnt[s_i] += 16
            instr.then_inc(eng.dma_sems[s_i], 16)
            tok = (eng.dma_sems[s_i], eng.dma_cnt[s_i])
        else:
            instr = build()
        if not dma:
            eng.n += 1
            instr.then_inc(eng.sem, 1)
            tok = (eng.sem, eng.n)
        for h in reads:
            self.buf(h).rtoks.append(tok)
        for h in writes:
            b = self.buf(h)
            b.wtok = tok
            b.rtoks = []
        return tok


def build_nc():
    nc = bass.Bass("TRN2", target_bir_lowering=False)
    TR = Tracker()

    x_h = nc.dram_tensor("x", [S, D], F32, kind="ExternalInput")
    wrT_h = nc.dram_tensor("wrT", [D, D], F32, kind="ExternalInput")
    wkT_h = nc.dram_tensor("wkT", [D, D], F32, kind="ExternalInput")
    wvT_h = nc.dram_tensor("wvT", [D, D], F32, kind="ExternalInput")
    woT_h = nc.dram_tensor("woT", [D, D], F32, kind="ExternalInput")
    wg_h = nc.dram_tensor("wg", [NF, D], F32, kind="ExternalInput")
    wu_h = nc.dram_tensor("wu", [NF, D], F32, kind="ExternalInput")
    wd_h = nc.dram_tensor("wd", [D, NF], F32, kind="ExternalInput")
    tmr_h = nc.dram_tensor("tmr", [128, NCh], F32, kind="ExternalInput")
    tmk_h = nc.dram_tensor("tmk", [128, NCh], F32, kind="ExternalInput")
    tmv_h = nc.dram_tensor("tmv", [128, NCh], F32, kind="ExternalInput")
    td_h = nc.dram_tensor("td", [128, NCh], F32, kind="ExternalInput")
    tf_h = nc.dram_tensor("tf", [128, NCh], F32, kind="ExternalInput")
    lng_h = nc.dram_tensor("lng", [128, NCh], F32, kind="ExternalInput")
    lnb_h = nc.dram_tensor("lnb", [128, NCh], F32, kind="ExternalInput")
    out_h = nc.dram_tensor("out", [S, D], F32, kind="ExternalOutput")

    ident_np = np.eye(128, dtype=bf16np)
    sel_np = np.zeros((128, 2), dtype=bf16np); sel_np[0:64, 0] = 1; sel_np[64:128, 1] = 1
    ident_d = nc.inline_tensor(np.asarray(ident_np).view(np.uint16), "ident_d")
    sel_d = nc.inline_tensor(np.asarray(sel_np).view(np.uint16), "sel_d")
    selT_d = nc.inline_tensor(np.ascontiguousarray(sel_np.T).view(np.uint16), "selT_d")

    xnbf_d = nc.dram_tensor("xnbf_d", [S, D], BF16, kind="Internal")
    xq_d = nc.dram_tensor("xq_d", [S, D], BF16, kind="Internal")
    xqT_d = nc.dram_tensor("xqT_d", [128, NCh * S], BF16, kind="Internal")
    wgq_d = nc.dram_tensor("wgq_d", [NF, D], BF16, kind="Internal")
    wuq_d = nc.dram_tensor("wuq_d", [NF, D], BF16, kind="Internal")
    wdq_d = nc.dram_tensor("wdq_d", [D, NF], BF16, kind="Internal")
    wdqT_d = nc.dram_tensor("wdqT_d", [128, NNC * D], BF16, kind="Internal")
    rwkv_d = nc.dram_tensor("rwkv_d", [D, S], BF16, kind="Internal")
    normed_d = nc.dram_tensor("normed_d", [D, S], BF16, kind="Internal")

    esg = contextlib.ExitStack()
    def mk_sb(stack):
        return lambda n, s_, d=F32: stack.enter_context(nc.sbuf_tensor(n, s_, d))
    sbg = mk_sb(esg)
    sem = lambda n: esg.enter_context(nc.semaphore(n))

    sv = sem("sv"); ss = sem("ss"); st_ = sem("st"); sg_ = sem("sg")
    NSL_DMA = 8
    d_sy = [sem(f"d_sy{i}") for i in range(NSL_DMA)]
    d_gp = [sem(f"d_gp{i}") for i in range(NSL_DMA)]
    d_sc = [sem(f"d_sc{i}") for i in range(NSL_DMA)]
    d_out = [sem(f"d_out{i}") for i in range(4)]
    V = Eng(nc, nc.vector, sv)
    SC = Eng(nc, nc.scalar, ss)
    T = Eng(nc, nc.tensor, st_)
    G = Eng(nc, nc.gpsimd, sg_)
    SY = Eng(nc, nc.sync, object(), dma_sems=d_sy)
    GQ = Eng(nc, nc.gpsimd, sg_, dma_sems=d_gp)
    SCQ = Eng(nc, nc.scalar, ss, dma_sems=d_sc)
    OQ = Eng(nc, nc.scalar, ss, dma_sems=d_out)
    GQ.watermark = G.watermark
    SCQ.watermark = SC.watermark
    OQ.watermark = SC.watermark
    TR.set_mutex((V, SC), (V, T))

    TPOSE = object()
    vop = lambda b, r=(), w=(): TR.op(V, b, r, w)
    sop = lambda b, r=(), w=(): TR.op(SC, b, r, w)
    top = lambda b, r=(), w=(): TR.op(T, b, r, w)
    gop = lambda b, r=(), w=(): TR.op(G, b, r, w)
    dma_sy = lambda b, r=(), w=(): TR.op(SY, b, r, w, dma=True)
    dma_gp = lambda b, r=(), w=(): TR.op(GQ, b, r, w, dma=True)
    dma_sc = lambda b, r=(), w=(): TR.op(SCQ, b, r, w, dma=True)
    dma_out = lambda b, r=(), w=(): TR.op(OQ, b, r, w, dma=True)

    all_sems = [sv, ss, st_, sg_, d_sy, d_gp, d_out]

    def fence():
        vals = {id(sv): V.n, id(ss): SC.n, id(st_): T.n, id(sg_): G.n}
        for Q in (SY, GQ, SCQ, OQ):
            for s_, c_ in zip(Q.dma_sems, Q.dma_cnt):
                if c_:
                    nc.vector.wait_ge(s_, c_)
                    vals[id(s_)] = c_
        nc.all_engine_barrier()
        for E in (V, SC, T, G, SY):
            E.watermark.update(vals)

    ident = sbg("ident", [128, 128], BF16)
    selb = sbg("selb", [128, 2], BF16)
    selTb = sbg("selTb", [2, 128], BF16)
    tmr = sbg("tmr_s", [128, NCh]); tmk = sbg("tmk_s", [128, NCh]); tmv = sbg("tmv_s", [128, NCh])
    td = sbg("td_s", [128, NCh]); tf = sbg("tf_s", [128, NCh])
    lng = sbg("lng_s", [128, NCh]); lnb = sbg("lnb_s", [128, NCh])
    e1t = sbg("e1t", [128, NCh]); ewt = sbg("ewt", [128, NCh]); ewit = sbg("ewit", [128, NCh])
    eut = sbg("eut", [128, NCh]); eumw = sbg("eumw", [128, NCh])
    statS = sbg("statS", [128, 16]); statQ = sbg("statQ", [128, 16])
    scl128 = sbg("scl128", [128, NCh]); bia128 = sbg("bia128", [128, NCh])
    X0 = sbg("X0", [128, QW]); X1 = sbg("X1", [128, QW])
    X2 = sbg("X2", [128, QW]); X3 = sbg("X3", [128, QW])
    Qb1 = sbg("Qb1", [128, QW // 16]); Qb2 = sbg("Qb2", [128, QW // 16])
    Qout = sbg("Qout", [128, QW], BF16)
    Qout2 = sbg("Qout2", [128, QW], BF16)
    junkw = sbg("junkw", [128, QW])
    TR.op(V, lambda: nc.vector.memset(junkw[:, :], 0.0), (), (junkw,))

    def vguard(build_out_in, r=(), w=(), wd=64):
        return TR.op(V, build_out_in, r, w)
    vrecip = vguard


    PA = esg.enter_context(nc.psum_tensor("PA", [128, NHF], F32))
    PB = esg.enter_context(nc.psum_tensor("PB", [128, NHF], F32))
    PC = esg.enter_context(nc.psum_tensor("PC", [128, D], F32))

    for dst, src in ((ident, ident_d), (selb, sel_d), (selTb, selT_d)):
        dma_sy((lambda dst=dst, src=src: nc.sync.dma_start(dst[:, :], src.ap().bitcast(BF16))),
               w=[dst])
    for dst, src in ((tmr, tmr_h), (tmk, tmk_h), (tmv, tmv_h), (td, td_h),
                     (tf, tf_h), (lng, lng_h), (lnb, lnb_h)):
        dma_sy((lambda dst=dst, src=src: nc.sync.dma_start(dst[:, :], src.ap())), w=[dst])
    sop(lambda: nc.scalar.activation(e1t[:, :], td[:, :], AF.Exp), r=[td], w=[e1t])
    sop(lambda: nc.scalar.activation(ewt[:, :], e1t[:, :], AF.Exp, scale=-1.0), r=[e1t], w=[ewt])
    sop(lambda: nc.scalar.activation(ewit[:, :], e1t[:, :], AF.Exp, scale=1.0), r=[e1t], w=[ewit])
    sop(lambda: nc.scalar.activation(eut[:, :], tf[:, :], AF.Exp), r=[tf], w=[eut])
    vop(lambda: nc.vector.tensor_sub(eumw[:, :], eut[:, :], ewit[:, :]), r=[eut, ewit], w=[eumw])

    def expand16(dst, src, nblk):
        dw = dst.shape[1]
        for j in range(16):
            vop(lambda j=j: nc.vector.tensor_copy(
                   bass.AP(dst, j, [[dw, 128], [16, nblk]]), src[:, 0:nblk]),
               r=[src], w=[dst])

    def quant(src_ap, src_h, W_, out_ap, out_h):
        nblk = W_ // 16
        src3 = bass.AP(src_h, src_ap.offset, [[src_h.shape[1], 128], [16, nblk], [1, 16]])
        vop(lambda: nc.vector.tensor_reduce(Qb1[:, 0:nblk], src3, AX.X, OP.max,
                                            apply_absolute_value=True), r=[src_h], w=[Qb1])
        vop(lambda: nc.vector.tensor_scalar(Qb2[:, 0:nblk], Qb1[:, 0:nblk], 1.0 / 6.0, 1e-12,
                                            OP.mult, OP.max), r=[Qb1], w=[Qb2])
        vop(lambda: nc.vector.reciprocal(Qb1[:, 0:nblk], Qb2[:, 0:nblk]), r=[Qb2], w=[Qb1])
        expand16(X2, Qb1, nblk)
        vop(lambda: nc.vector.tensor_mul(X1[:, 0:W_], src_ap, X2[:, 0:W_]),
            r=[src_h, X2], w=[X1])
        vop(lambda: nc.vector.tensor_mul(X2[:, 0:W_], X1[:, 0:W_], X1[:, 0:W_]),
            r=[X1], w=[X2])
        MID2 = (0.0625, 0.5625, 1.5625, 3.0625, 6.25, 12.25, 25.0)
        DLT = (0.5, 0.5, 0.5, 0.5, 1.0, 1.0, 2.0)
        vop(lambda: nc.vector.tensor_scalar(X3[:, 0:W_], X2[:, 0:W_], MID2[0], DLT[0],
                                            OP.is_ge, OP.mult), r=[X2], w=[X3])
        for j in range(1, 7):
            vop(lambda j=j: nc.vector.tensor_scalar(junkw[:, 0:W_], X2[:, 0:W_], MID2[j],
                                                    DLT[j], OP.is_ge, OP.mult),
                r=[X2], w=[junkw])
            vop(lambda: nc.vector.tensor_add(X3[:, 0:W_], X3[:, 0:W_], junkw[:, 0:W_]),
                r=[X3, junkw], w=[X3])
        vop(lambda: nc.vector.tensor_scalar(X2[:, 0:W_], X1[:, 0:W_], 0.0, 2.0,
                                            OP.is_ge, OP.mult), r=[X1], w=[X2])
        vop(lambda: nc.vector.tensor_scalar_add(X2[:, 0:W_], X2[:, 0:W_], -1.0),
            r=[X2], w=[X2])
        vop(lambda: nc.vector.tensor_mul(X3[:, 0:W_], X3[:, 0:W_], X2[:, 0:W_]),
            r=[X3, X2], w=[X3])
        expand16(X2, Qb2, nblk)
        vop(lambda: nc.vector.tensor_mul(out_ap, X3[:, 0:W_], X2[:, 0:W_]),
            r=[X3, X2], w=[out_h])

    es1 = contextlib.ExitStack(); sb1 = mk_sb(es1)
    xa = sb1("xa", [128, D]); xb2 = sb1("xb2", [128, D])
    xnf = sb1("xnf", [128, D]); xnb = sb1("xnb", [128, D], BF16)
    lnScr = sb1("lnScr", [128, D])
    lnS = sb1("lnS", [128, 1]); lnQ = sb1("lnQ", [128, 1])
    lnT1 = sb1("lnT1", [128, 1]); lnT2 = sb1("lnT2", [128, 1])
    lnR = sb1("lnR", [128, 1]); lnB = sb1("lnB", [128, 1])
    wnat = [sb1("wnatA", [128, NF]), sb1("wnatB", [128, NF])]
    tstage = sb1("tstage", [128, S], BF16)

    xtiles = [xa, xb2]
    for i in range(NT):
        xt_ = xtiles[i % 2]
        dma_sy(lambda i=i, xt_=xt_: nc.sync.dma_start(xt_[:, :],
                                                      x_h.ap()[128 * i:128 * (i + 1), :]),
               w=[xt_])
        vguard(lambda xt_=xt_: nc.vector.reduce_sum(lnS[:, :], xt_[:, :], AX.X),
            r=[xt_], w=[lnS], wd=D)
        sop(lambda xt_=xt_: nc.scalar.activation(X3[:, 0:D], xt_[:, :], AF.Square), r=[xt_], w=[X3])
        vguard(lambda: nc.vector.reduce_sum(lnQ[:, :], X3[:, 0:D], AX.X), r=[X3], w=[lnQ], wd=D)
        vop(lambda: nc.vector.tensor_scalar_mul(lnT1[:, :], lnS[:, :], 1.0 / D), r=[lnS], w=[lnT1])
        vop(lambda: nc.vector.tensor_scalar_mul(lnT2[:, :], lnQ[:, :], 1.0 / D), r=[lnQ], w=[lnT2])
        vop(lambda: nc.vector.tensor_mul(lnS[:, :], lnT1[:, :], lnT1[:, :]), r=[lnT1], w=[lnS])
        vop(lambda: nc.vector.tensor_sub(lnT2[:, :], lnT2[:, :], lnS[:, :]), r=[lnT2, lnS], w=[lnT2])
        vop(lambda: nc.vector.tensor_scalar_add(lnT2[:, :], lnT2[:, :], 1e-5), r=[lnT2], w=[lnT2])
        sop(lambda: nc.scalar.activation(lnT2[:, :], lnT2[:, :], AF.Ln), r=[lnT2], w=[lnT2])
        sop(lambda: nc.scalar.activation(lnR[:, :], lnT2[:, :], AF.Exp, scale=-0.5),
            r=[lnT2], w=[lnR])
        vop(lambda: nc.vector.scalar_tensor_tensor(lnB[:, :], lnT1[:, :], -1.0, lnR[:, :],
                                                   OP.mult, OP.mult), r=[lnT1, lnR], w=[lnB])
        vop(lambda xt_=xt_: nc.vector.tensor_scalar(xnf[:, :], xt_[:, :], lnR[:, :],
                                                    lnB[:, :], OP.mult, OP.add),
            r=[xt_, lnR, lnB], w=[xnf])
        sop(lambda: nc.scalar.copy(xnb[:, :], xnf[:, :]), r=[xnf], w=[xnb])
        dma_sc(lambda i=i: nc.scalar.dma_start(xnbf_d.ap()[128 * i:128 * (i + 1), :],
                                               xnb[:, :]), r=[xnb], w=[xnbf_d])
        quant(xnf[:, 0:D], xnf, D, Qout[:, 0:D], Qout)
        dma_sc(lambda i=i: nc.scalar.dma_start(xq_d.ap()[128 * i:128 * (i + 1), :],
                                               Qout[:, 0:D]), r=[Qout], w=[xq_d])

    fence()
    for j in range(NCh):
        dma_sy(lambda j=j: nc.sync.dma_start(tstage[:, :], xq_d.ap()[:, 128 * j:128 * (j + 1)],
                                             transpose=True), r=[xq_d, TPOSE], w=[tstage, TPOSE])
        dma_sc(lambda j=j: nc.scalar.dma_start(xqT_d.ap()[:, S * j:S * (j + 1)], tstage[:, :]),
               r=[tstage], w=[xqT_d])
    fence()
    fence()
    for w_h, wq_dr, nrows, rowW in ((wg_h, wgq_d, NF, D), (wu_h, wuq_d, NF, D),
                                    (wd_h, wdq_d, D, NF)):
        for c in range(nrows // 128):
            wn = wnat[c % 2]
            dma_sy(lambda w_h=w_h, c=c, wn=wn, rowW=rowW:
                   nc.sync.dma_start(wn[:, 0:rowW], w_h.ap()[128 * c:128 * (c + 1), :]), w=[wn])
            nh = 1 if rowW <= QW else 2
            for hh in range(nh):
                w0 = (rowW // nh) * hh
                ww = rowW // nh
                quant(wn[:, w0:w0 + ww], wn, ww, Qout[:, 0:ww], Qout)
                dma_sc(lambda wq_dr=wq_dr, c=c, w0=w0, ww=ww:
                       nc.scalar.dma_start(wq_dr.ap()[128 * c:128 * (c + 1), w0:w0 + ww],
                                           Qout[:, 0:ww]), r=[Qout], w=[wq_dr])

    es1.close()

    es23 = contextlib.ExitStack(); sb23 = mk_sb(es23)
    xrT = [sb23(f"xrT{j}", [128, S], BF16) for j in range(NCh)]
    xkT = [sb23(f"xkT{j}", [128, S], BF16) for j in range(NCh)]
    xvT = [sb23(f"xvT{j}", [128, S], BF16) for j in range(NCh)]
    es2 = contextlib.ExitStack(); sb2 = mk_sb(es2)
    OFF = 16
    xnT = [sb2(f"xnT{j}", [128, S + OFF], BF16) for j in range(NCh)]
    dmix = sb2("dmix", [128, S], BF16)
    for j in range(NCh):
        xj = xnT[j]
        vop(lambda xj=xj: nc.vector.memset(xj[:, OFF - 1:OFF], 0.0), w=[xj])
        dma_sy(lambda j=j, xj=xj: nc.sync.dma_start(
                   xj[:, OFF:S + OFF], xnbf_d.ap()[:, 128 * j:128 * (j + 1)], transpose=True),
               r=[xnbf_d, TPOSE], w=[xj, TPOSE])
        vop(lambda xj=xj: nc.vector.tensor_sub(dmix[:, :], xj[:, OFF:S + OFF],
                                               xj[:, OFF - 1:S + OFF - 1]), r=[xj], w=[dmix])
        for lam, dst in ((tmr, xrT[j]), (tmk, xkT[j]), (tmv, xvT[j])):
            vop(lambda lam=lam, dst=dst, xj=xj, j=j:
                nc.vector.scalar_tensor_tensor(dst[:, :], dmix[:, :], lam[:, j:j + 1],
                                               xj[:, OFF - 1:S + OFF - 1], OP.mult, OP.add),
                r=[dmix, lam, xj], w=[dst])
    fence()
    es2.close()

    es3 = contextlib.ExitStack(); sb3 = mk_sb(es3)
    wsl = {w: [sb3(f"wsl_{w}{b}", [128, D], BF16) for b in range(2)] for w in "rkv"}
    wslf = {w: [sb3(f"wslf_{w}{b}", [128, D]) for b in range(2)] for w in "rkv"}
    wsrc = {"r": wrT_h, "k": wkT_h, "v": wvT_h}
    sgr = sb3("sgr", [128, HALF])
    f_t = sb3("f_t", [128, HALF])
    ewfull = sb3("ewfull", [128, HALF])
    rwkvf = sb3("rwkvf", [128, HALF])
    rwkvb = sb3("rwkvb", [128, HALF], BF16)
    B0t = sb3("B0t", [128, HALF + 1]); Bt = sb3("Bt", [128, HALF + 1])
    gt = sb3("gt", [128, HALF + 1]); At = sb3("At", [128, HALF + 1])
    cB0 = sb3("cB0", [128, 1]); cB = sb3("cB", [128, 1]); cA = sb3("cA", [128, 1])
    ekt = lambda: X0[:, 0:HALF]
    ekvt = lambda: X1[:, 0:HALF]
    sc1 = lambda: X2[:, 0:HALF]
    sc2 = lambda: X3[:, 0:HALF]
    r_ps = lambda: PA.ap()[:, 0:HALF]
    k_ps = lambda: PB.ap()[:, 0:HALF]
    v_ps = lambda: PC.ap()[:, 0:HALF]

    for c in range(NCh):
        vop(lambda: nc.vector.memset(ewfull[:, :], 1.0), w=[ewfull])
        vop(lambda c=c: nc.vector.tensor_scalar_mul(ewfull[:, :], ewfull[:, :], ewt[:, c:c + 1]),
            r=[ewfull, ewt], w=[ewfull])
        for wname in "rkv":
            wb = wsl[wname][c % 2]
            wf_ = wslf[wname][c % 2]
            dma_sy(lambda wname=wname, c=c, wf_=wf_: nc.sync.dma_start(
                       wf_.ap().rearrange("p (f q) -> p f q", f=NCh),
                       wsrc[wname].ap().rearrange("(f p) q -> p f q", p=128)
                       [:, :, 128 * c:128 * (c + 1)]), w=[wf_])
            sop(lambda wb=wb, wf_=wf_: nc.scalar.copy(wb[:, :], wf_[:, :]), r=[wf_], w=[wb])
        for t_ in (cB0, cB, cA):
            vop(lambda t_=t_: nc.vector.memset(t_[:, :], 0.0), w=[t_])
        vop(lambda: nc.vector.memset(B0t[:, 0:1], 0.0), w=[B0t])
        vop(lambda: nc.vector.memset(Bt[:, 0:1], 0.0), w=[Bt])
        vop(lambda: nc.vector.memset(gt[:, 0:1], 1.0), w=[gt])

        for th in range(2):
            t0 = th * HALF
            for wname, pst, psum_t, xsrc in (("r", r_ps, PA, xrT), ("k", k_ps, PB, xkT),
                                             ("v", v_ps, PC, xvT)):
                wb = wsl[wname][c % 2]
                for fj in range(NCh):
                    for nsl in range(2):
                        top(lambda wb=wb, fj=fj, nsl=nsl, pst=pst, xsrc=xsrc, t0=t0:
                            nc.tensor.matmul(pst()[:, 512 * nsl:512 * (nsl + 1)],
                                             wb.ap().rearrange("p (f q) -> p f q", f=NCh)[:, fj, :],
                                             xsrc[fj][:, t0 + 512 * nsl:t0 + 512 * (nsl + 1)],
                                             start=(fj == 0), stop=(fj == NCh - 1)),
                            r=[wb, xsrc[fj]], w=[psum_t])
            sop(lambda: nc.scalar.activation(ekt(), k_ps(), AF.Exp), r=[PB], w=[X0])
            vop(lambda: nc.vector.tensor_mul(ekvt(), ekt(), v_ps()), r=[X0, PC], w=[X1])
            sop(lambda: nc.scalar.activation(sgr[:, :], r_ps(), AF.Exp, scale=-1.0),
                r=[PA], w=[sgr])
            vop(lambda: nc.vector.tensor_scalar_add(sgr[:, :], sgr[:, :], 1.0), r=[sgr], w=[sgr])
            vrecip(lambda: nc.vector.reciprocal(sgr[:, :], sgr[:, :]), r=[sgr], w=[sgr], wd=HALF)
            vop(lambda: nc.vector.tensor_tensor_scan(B0t[:, 1:], ewfull[:, :], ekt(),
                                                     cB0[:, :], OP.mult, OP.add),
                r=[ewfull, X0, cB0], w=[B0t])
            vop(lambda: nc.vector.tensor_copy(cB0[:, :], B0t[:, HALF:]), r=[B0t], w=[cB0])
            vop(lambda c=c: nc.vector.tensor_scalar_mul(sc1(), B0t[:, 0:HALF], ewt[:, c:c + 1]),
                r=[B0t, ewt], w=[X2])
            vop(lambda: nc.vector.tensor_max(sc1(), sc1(), ekt()), r=[X2, X0], w=[X2])
            vrecip(lambda: nc.vector.reciprocal(sc2(), B0t[:, 1:]), r=[B0t], w=[X3], wd=HALF)
            vop(lambda: nc.vector.tensor_mul(sc1(), sc1(), sc2()), r=[X2, X3], w=[X2])
            vop(lambda: nc.vector.tensor_scalar(f_t[:, :], sc1(), EPS, 1.0, OP.mult, OP.add),
                r=[X2], w=[f_t])
            vop(lambda c=c: nc.vector.tensor_scalar_mul(sc1(), f_t[:, :], ewt[:, c:c + 1]),
                r=[f_t, ewt], w=[X2])
            vop(lambda: nc.vector.tensor_mul(sc2(), f_t[:, :], ekt()), r=[f_t, X0], w=[X3])
            vop(lambda: nc.vector.tensor_tensor_scan(Bt[:, 1:], sc1(), sc2(), cB[:, :],
                                                     OP.mult, OP.add), r=[X2, X3, cB], w=[Bt])
            vop(lambda: nc.vector.tensor_copy(cB[:, :], Bt[:, HALF:]), r=[Bt], w=[cB])
            vop(lambda c=c: nc.vector.tensor_scalar_mul(sc1(), Bt[:, 0:HALF], ewt[:, c:c + 1]),
                r=[Bt, ewt], w=[X2])
            vop(lambda: nc.vector.tensor_max(sc1(), sc1(), ekt()), r=[X2, X0], w=[X2])
            vrecip(lambda: nc.vector.reciprocal(sc1(), sc1()), r=[X2], w=[X2], wd=HALF)
            vop(lambda: nc.vector.tensor_mul(gt[:, 1:], Bt[:, 1:], sc1()), r=[Bt, X2], w=[gt])
            vop(lambda c=c: nc.vector.tensor_scalar_mul(sc1(), gt[:, 0:HALF], ewt[:, c:c + 1]),
                r=[gt, ewt], w=[X2])
            vop(lambda: nc.vector.tensor_tensor_scan(At[:, 1:], sc1(), ekvt(), cA[:, :],
                                                     OP.mult, OP.add), r=[X2, X1, cA], w=[At])
            vop(lambda: nc.vector.tensor_copy(cA[:, :], At[:, HALF:]), r=[At], w=[cA])
            vop(lambda: nc.vector.tensor_copy(gt[:, 0:1], gt[:, HALF:]), r=[gt], w=[gt])
            vop(lambda c=c: nc.vector.tensor_scalar_mul(sc1(), At[:, 1:], ewit[:, c:c + 1]),
                r=[At, ewit], w=[X2])
            vop(lambda c=c: nc.vector.scalar_tensor_tensor(sc1(), ekvt(), eumw[:, c:c + 1],
                                                           sc1(), OP.mult, OP.add),
                r=[X1, eumw, X2], w=[X2])
            vop(lambda c=c: nc.vector.tensor_scalar_mul(ekt(), ekt(), eut[:, c:c + 1]),
                r=[X0, eut], w=[X0])
            vop(lambda: nc.vector.tensor_max(sc2(), Bt[:, 0:HALF], ekt()), r=[Bt, X0], w=[X3])
            vop(lambda: nc.vector.tensor_add(ekt(), Bt[:, 0:HALF], ekt()), r=[Bt, X0], w=[X0])
            vop(lambda: nc.vector.scalar_tensor_tensor(sc2(), sc2(), EPS, ekt(),
                                                       OP.mult, OP.add), r=[X3, X0], w=[X3])
            vrecip(lambda: nc.vector.reciprocal(sc2(), sc2()), r=[X3], w=[X3], wd=HALF)
            vop(lambda: nc.vector.tensor_mul(sc1(), sc1(), sc2()), r=[X2, X3], w=[X2])
            col = 2 * c + th
            vop(lambda: nc.vector.tensor_mul(rwkvf[:, :], sc1(), sgr[:, :]),
                r=[X2, sgr], w=[rwkvf])
            vguard(lambda col=col: nc.vector.reduce_sum(statS[:, col:col + 1], rwkvf[:, :], AX.X),
                r=[rwkvf], w=[statS], wd=HALF)
            sop(lambda: nc.scalar.activation(ekvt(), rwkvf[:, :], AF.Square), r=[rwkvf], w=[X1])
            vguard(lambda col=col: nc.vector.reduce_sum(statQ[:, col:col + 1], ekvt(), AX.X),
                r=[X1], w=[statQ], wd=HALF)
            sop(lambda: nc.scalar.copy(rwkvb[:, :], rwkvf[:, :]), r=[rwkvf], w=[rwkvb])
            dma_sc(lambda c=c, t0=t0: nc.scalar.dma_start(
                       rwkv_d.ap()[128 * c:128 * (c + 1), t0:t0 + HALF], rwkvb[:, :]),
                   r=[rwkvb], w=[rwkv_d])
            vop(lambda: nc.vector.tensor_copy(B0t[:, 0:1], cB0[:, :]), r=[cB0], w=[B0t])
            vop(lambda: nc.vector.tensor_copy(Bt[:, 0:1], cB[:, :]), r=[cB], w=[Bt])
    fence()
    es3.close()
    es23.close()

    es4 = contextlib.ExitStack(); sb4 = mk_sb(es4)
    gs8 = sb4("gs8", [128, NCh]); gq8 = sb4("gq8", [128, NCh])
    hi_b = sb4("hi_b", [128, 2 * NCh], BF16); lo_b = sb4("lo_b", [128, 2 * NCh], BF16)
    lo_f = sb4("lo_f", [128, 2 * NCh])
    tot2 = sb4("tot2", [2, 2 * NCh]); tot2b = sb4("tot2b", [2, 2 * NCh], BF16)
    sall = sb4("sall", [128, NCh]); qall = sb4("qall", [128, NCh])
    gtmp1 = sb4("gtmp1", [128, NCh]); gtmp2 = sb4("gtmp2", [128, NCh])
    rload = sb4("rload", [128, S], BF16)
    nstage = sb4("nstage", [128, S], BF16)

    s_str = lambda tns, off: bass.AP(tns, off, [[16, 128], [2, NCh]])
    vop(lambda: nc.vector.tensor_add(gs8[:, :], s_str(statS, 0), s_str(statS, 1)),
        r=[statS], w=[gs8])
    vop(lambda: nc.vector.tensor_add(gq8[:, :], s_str(statQ, 0), s_str(statQ, 1)),
        r=[statQ], w=[gq8])
    vop(lambda: nc.vector.tensor_copy(hi_b[:, 0:NCh], gs8[:, :]), r=[gs8], w=[hi_b])
    vop(lambda: nc.vector.tensor_copy(hi_b[:, NCh:], gq8[:, :]), r=[gq8], w=[hi_b])
    vop(lambda: nc.vector.tensor_sub(lo_f[:, 0:NCh], gs8[:, :], hi_b[:, 0:NCh]),
        r=[gs8, hi_b], w=[lo_f])
    vop(lambda: nc.vector.tensor_sub(lo_f[:, NCh:], gq8[:, :], hi_b[:, NCh:]),
        r=[gq8, hi_b], w=[lo_f])
    vop(lambda: nc.vector.tensor_copy(lo_b[:, :], lo_f[:, :]), r=[lo_f], w=[lo_b])
    top(lambda: nc.tensor.matmul(PC.ap()[0:2, 0:2 * NCh], selb[:, :], hi_b[:, :],
                                 start=True, stop=False), r=[selb, hi_b], w=[PC])
    top(lambda: nc.tensor.matmul(PC.ap()[0:2, 0:2 * NCh], selb[:, :], lo_b[:, :],
                                 start=False, stop=True), r=[selb, lo_b], w=[PC])
    vop(lambda: nc.vector.tensor_copy(tot2[:, :], PC.ap()[0:2, 0:2 * NCh]), r=[PC], w=[tot2])
    vop(lambda: nc.vector.tensor_copy(tot2b[:, :], tot2[:, :]), r=[tot2], w=[tot2b])
    top(lambda: nc.tensor.matmul(PC.ap()[:, 0:2 * NCh], selTb[:, :], tot2b[:, :],
                                 start=True, stop=True), r=[selTb, tot2b], w=[PC])
    vop(lambda: nc.vector.tensor_copy(sall[:, :], PC.ap()[:, 0:NCh]), r=[PC], w=[sall])
    vop(lambda: nc.vector.tensor_copy(qall[:, :], PC.ap()[:, NCh:2 * NCh]), r=[PC], w=[qall])
    NEL = float(S * HD)
    vop(lambda: nc.vector.tensor_scalar_mul(sall[:, :], sall[:, :], 1.0 / NEL), r=[sall], w=[sall])
    vop(lambda: nc.vector.tensor_scalar_mul(qall[:, :], qall[:, :], 1.0 / NEL), r=[qall], w=[qall])
    vop(lambda: nc.vector.tensor_mul(gtmp1[:, :], sall[:, :], sall[:, :]), r=[sall], w=[gtmp1])
    vop(lambda: nc.vector.tensor_sub(qall[:, :], qall[:, :], gtmp1[:, :]),
        r=[qall, gtmp1], w=[qall])
    vop(lambda: nc.vector.tensor_scalar_add(qall[:, :], qall[:, :], 1e-5), r=[qall], w=[qall])
    sop(lambda: nc.scalar.activation(qall[:, :], qall[:, :], AF.Ln), r=[qall], w=[qall])
    sop(lambda: nc.scalar.activation(gtmp2[:, :], qall[:, :], AF.Exp, scale=-0.5),
        r=[qall], w=[gtmp2])
    vop(lambda: nc.vector.tensor_mul(scl128[:, :], gtmp2[:, :], lng[:, :]),
        r=[gtmp2, lng], w=[scl128])
    vop(lambda: nc.vector.tensor_mul(gtmp1[:, :], sall[:, :], scl128[:, :]),
        r=[sall, scl128], w=[gtmp1])
    vop(lambda: nc.vector.tensor_sub(bia128[:, :], lnb[:, :], gtmp1[:, :]),
        r=[lnb, gtmp1], w=[bia128])
    for c in range(NCh):
        dma_sy(lambda c=c: nc.sync.dma_start(rload[:, :], rwkv_d.ap()[128 * c:128 * (c + 1), :]),
               r=[rwkv_d], w=[rload])
        vop(lambda c=c: nc.vector.tensor_scalar(nstage[:, :], rload[:, :], scl128[:, c:c + 1],
                                                bia128[:, c:c + 1], OP.mult, OP.add),
            r=[rload, scl128, bia128], w=[nstage])
        dma_sc(lambda c=c: nc.scalar.dma_start(normed_d.ap()[128 * c:128 * (c + 1), :],
                                               nstage[:, :]), r=[nstage], w=[normed_d])
    fence()
    es4.close()

    es5 = contextlib.ExitStack(); sb5 = mk_sb(es5)
    wgT = [sb5(f"wgT{j}", [128, NF], BF16) for j in range(NCh)]
    wuT = [sb5(f"wuT{j}", [128, NF], BF16) for j in range(NCh)]
    woT = [sb5(f"woT{j}", [128, D], BF16) for j in range(NCh)]
    wdT_st = sb5("wdT_st", [128, D], BF16)
    wdq_sl = sb5("wdq_sl", [128, (NNC // 2) * D], BF16)
    xqT_sl = [sb5("xqT_A", [128, NCh, 128], BF16), sb5("xqT_B", [128, NCh, 128], BF16)]
    nrm_sl = [sb5("nrm_A", [128, NCh, 128], BF16), sb5("nrm_B", [128, NCh, 128], BF16)]
    x_sl = [sb5("x_slA", [128, D]), sb5("x_slB", [128, D])]

    for j in range(NCh):
        dma_sy(lambda j=j: nc.sync.dma_start(wgT[j][:, :], wgq_d.ap()[:, 128 * j:128 * (j + 1)],
                                             transpose=True), r=[wgq_d, TPOSE], w=[wgT[j], TPOSE])
        dma_sy(lambda j=j: nc.sync.dma_start(wuT[j][:, :], wuq_d.ap()[:, 128 * j:128 * (j + 1)],
                                             transpose=True), r=[wuq_d, TPOSE], w=[wuT[j], TPOSE])
    for nj in range(NNC):
        dma_sy(lambda nj=nj: nc.sync.dma_start(wdT_st[:, :],
                                               wdq_d.ap()[:, 128 * nj:128 * (nj + 1)],
                                               transpose=True), r=[wdq_d, TPOSE], w=[wdT_st, TPOSE])
        dma_sc(lambda nj=nj: nc.scalar.dma_start(wdqT_d.ap()[:, D * nj:D * (nj + 1)],
                                                 wdT_st[:, :]), r=[wdT_st], w=[wdqT_d])
    fence()
    woTf = sb5("woTf", [128, D])
    for j in range(NCh):
        dma_sy(lambda j=j: nc.sync.dma_start(woTf[:, :],
                                             woT_h.ap()[128 * j:128 * (j + 1), :]), w=[woTf])
        sop(lambda j=j: nc.scalar.copy(woT[j][:, :], woTf[:, :]), r=[woTf], w=[woT[j]])

    gate_ps = lambda: PA.ap()[:, 0:NHF]
    up_ps = lambda: PB.ap()[:, 0:NHF]
    down_ps = lambda: PC.ap()
    hqT_ps = lambda: PA.ap().bitcast(BF16)[:, 0:NHF]
    s_ps = lambda: PB.ap()[:, 0:D]
    NSL = ((0, 512), (512, 1024), (1024, NHF))

    for i in range(NT):
        t0 = 128 * i
        xq_b = xqT_sl[i % 2]
        nr_b = nrm_sl[i % 2]
        xx = x_sl[i % 2]
        dma_sy(lambda i=i, xq_b=xq_b: nc.sync.dma_start(
                   xq_b[:, :, :],
                   xqT_d.ap().rearrange("p (a t) -> p a t", a=NCh)[:, :, t0:t0 + 128]),
               r=[xqT_d], w=[xq_b])
        dma_sy(lambda i=i, nr_b=nr_b: nc.sync.dma_start(
                   nr_b[:, :, :],
                   normed_d.ap().rearrange("(a p) t -> p a t", p=128)[:, :, t0:t0 + 128]),
               r=[normed_d], w=[nr_b])
        dma_sy(lambda i=i, xx=xx: nc.sync.dma_start(xx[:, :], x_h.ap()[t0:t0 + 128, :]),
               w=[xx])
        for hh in range(2):
            n0 = NHF * hh
            dma_sy(lambda hh=hh: nc.sync.dma_start(
                       wdq_sl[:, :], wdqT_d.ap()[:, (NNC // 2) * D * hh:(NNC // 2) * D * (hh + 1)]),
                   r=[wdqT_d], w=[wdq_sl])
            for fj in range(NCh):
                for (a, b) in NSL:
                    top(lambda fj=fj, a=a, b=b, xq_b=xq_b, n0=n0:
                        nc.tensor.matmul(gate_ps()[:, a:b], xq_b[:, fj, :],
                                         wgT[fj][:, n0 + a:n0 + b],
                                         start=(fj == 0), stop=(fj == NCh - 1)),
                        r=[xq_b, wgT[fj]], w=[PA])
                for (a, b) in NSL:
                    top(lambda fj=fj, a=a, b=b, xq_b=xq_b, n0=n0:
                        nc.tensor.matmul(up_ps()[:, a:b], xq_b[:, fj, :],
                                         wuT[fj][:, n0 + a:n0 + b],
                                         start=(fj == 0), stop=(fj == NCh - 1)),
                        r=[xq_b, wuT[fj]], w=[PB])
            sop(lambda: nc.scalar.activation(X2[:, 0:NHF], gate_ps(), AF.Silu), r=[PA], w=[X2])
            vop(lambda: nc.vector.tensor_mul(X0[:, 0:NHF], X2[:, 0:NHF], up_ps()),
                r=[X2, PB], w=[X0])
            quant(X0[:, 0:NHF], X0, NHF, Qout[:, 0:NHF], Qout)
            for pj in range(NHF // 128):
                top(lambda pj=pj: nc.tensor.transpose(
                        hqT_ps()[:, 128 * pj:128 * (pj + 1)], Qout[:, 128 * pj:128 * (pj + 1)],
                        ident[:, :]), r=[Qout, ident], w=[PA])
            sop(lambda: nc.scalar.copy(Qout2[:, 0:NHF], hqT_ps()), r=[PA], w=[Qout2])
            for pj in range(NHF // 128):
                nj = hh * (NHF // 128) + pj
                for (a, b) in ((0, 512), (512, 1024)):
                    top(lambda pj=pj, nj=nj, a=a, b=b:
                        nc.tensor.matmul(down_ps()[:, a:b], Qout2[:, 128 * pj:128 * (pj + 1)],
                                         wdq_sl.ap().rearrange("p (n q) -> p n q", n=NNC // 2)
                                         [:, pj, a:b],
                                         start=(nj == 0), stop=(nj == NNC - 1)),
                        r=[Qout2, wdq_sl], w=[PC])
        sop(lambda: nc.scalar.activation(X1[:, 0:D], down_ps(), AF.Silu), r=[PC], w=[X1])
        for cj in range(NCh):
            for (a, b) in ((0, 512), (512, 1024)):
                top(lambda cj=cj, a=a, b=b, nr_b=nr_b:
                    nc.tensor.matmul(s_ps()[:, a:b], nr_b[:, cj, :], woT[cj][:, a:b],
                                     start=(cj == 0), stop=(cj == NCh - 1)),
                    r=[nr_b, woT[cj]], w=[PB])
        vop(lambda: nc.vector.tensor_mul(X3[:, 0:D], X1[:, 0:D], s_ps()), r=[X1, PB], w=[X3])
        vop(lambda xx=xx: nc.vector.tensor_add(X3[:, 0:D], X3[:, 0:D], xx[:, :]),
            r=[X3, xx], w=[X3])
        dma_out(lambda i=i: nc.scalar.dma_start(out_h.ap()[t0:t0 + 128, :], X3[:, 0:D]),
                r=[X3])

    for Q in (SCQ, OQ):
        for s_, c_ in zip(Q.dma_sems, Q.dma_cnt):
            if c_:
                nc.sync.wait_ge(s_, c_)
    es5.close()
    esg.close()
    return nc


_cached = {}

def kernel(L0_attn_x, L0_attn_time_mix_r, L0_attn_time_mix_k, L0_attn_time_mix_v,
           L0_attn_W_r, L0_attn_W_k, L0_attn_W_v, L0_attn_W_o,
           L0_attn_time_decay, L0_attn_time_first, L0_attn_ln_gamma, L0_attn_ln_beta,
           L0_ffn_w_gate, L0_ffn_w_up, L0_ffn_w_down):
    x = np.ascontiguousarray(np.asarray(L0_attn_x, dtype=np.float32))
    B = x.shape[0]
    f32c = lambda a: np.ascontiguousarray(np.asarray(a, dtype=np.float32))
    tT = lambda a: np.ascontiguousarray(np.asarray(a, dtype=np.float32).T)
    pp = lambda a: np.ascontiguousarray(
        np.asarray(a, dtype=np.float32).reshape(-1).reshape(NCh, 128).T)

    shared = {
        "wrT": tT(L0_attn_W_r), "wkT": tT(L0_attn_W_k), "wvT": tT(L0_attn_W_v),
        "woT": tT(L0_attn_W_o),
        "wg": f32c(L0_ffn_w_gate), "wu": f32c(L0_ffn_w_up), "wd": f32c(L0_ffn_w_down),
        "tmr": pp(L0_attn_time_mix_r), "tmk": pp(L0_attn_time_mix_k),
        "tmv": pp(L0_attn_time_mix_v),
        "td": pp(L0_attn_time_decay), "tf": pp(L0_attn_time_first),
        "lng": pp(L0_attn_ln_gamma), "lnb": pp(L0_attn_ln_beta),
    }
    if "nc" not in _cached:
        _cached["nc"] = build_nc()
    nc = _cached["nc"]
    in_maps = [dict(shared, x=np.ascontiguousarray(x[b])) for b in range(B)]
    res = run_bass_kernel_spmd(nc, in_maps, core_ids=list(range(B)))
    return np.stack([r["out"] for r in res.results]).astype(np.float32)




# version 03
# speedup vs baseline: 1.0426x; 1.0426x over previous
import contextlib
import numpy as np
import ml_dtypes
import concourse.bass as bass
import concourse.mybir as mybir
from concourse.bass_utils import run_bass_kernel_spmd

F32 = mybir.dt.float32
BF16 = mybir.dt.bfloat16
I32 = mybir.dt.int32
AF = mybir.ActivationFunctionType
OP = mybir.AluOpType
AX = mybir.AxisListType

S = 2048
D = 1024
H = 16
HD = 64
NF = 2816
NT = S // 128
NCh = D // 128
NNC = NF // 128
HALF = 1024
NHF = NF // 2
QW = NHF
EPS = 1e-8
bf16np = ml_dtypes.bfloat16


class Eng:
    def __init__(self, nc, eng, sem, dma_sems=None):
        self.nc, self.eng, self.sem = nc, eng, sem
        self.n = 0
        self.dma_sems = dma_sems or []
        self.dma_cnt = [0] * len(self.dma_sems)
        self.dma_i = 0
        self.watermark = {}

    def wait_tok(self, tok):
        sem, val = tok
        key = id(sem)
        if self.watermark.get(key, -1) >= val:
            return
        self.eng.wait_ge(sem, val)
        self.watermark[key] = val


class Buf:
    __slots__ = ("wtok", "rtoks")
    def __init__(self):
        self.wtok = None
        self.rtoks = []


class Tracker:
    def __init__(self):
        self.bufs = {}
        self.drainers = {}

    def add_drainer(self, eng, junk_fn):
        self.drainers[id(eng.sem)] = [eng, junk_fn, -1, -1]

    def launder(self, consumer, tok):
        sem, val = tok
        if sem is consumer.sem:
            return tok
        d = self.drainers.get(id(sem))
        if d is None:
            return tok
        eng, junk_fn, covered, junk_val = d
        if covered >= val:
            return (sem, junk_val)
        pre = eng.n
        n0 = eng.n
        junk_fn()
        junk_fn()
        d[2] = pre
        d[3] = eng.n
        assert eng.n > n0
        return (sem, eng.n)

    def buf(self, h):
        k = id(h)
        if k not in self.bufs:
            self.bufs[k] = Buf()
        return self.bufs[k]

    def set_mutex(self, *pairs):
        self.mutex = pairs

    def op(self, eng, build, reads=(), writes=(), dma=False):
        for a, b in getattr(self, "mutex", ()):
            if eng is a and b.n:
                eng.wait_tok((b.sem, b.n))
            elif eng is b and a.n:
                eng.wait_tok((a.sem, a.n))
        toks = []
        for h in reads:
            b = self.buf(h)
            if b.wtok is not None:
                toks.append(b.wtok)
        for h in writes:
            b = self.buf(h)
            if b.wtok is not None:
                toks.append(b.wtok)
            toks.extend(b.rtoks)
        for t in toks:
            eng.wait_tok(self.launder(eng, t))
        if dma:
            s_i = eng.dma_i % len(eng.dma_sems)
            eng.dma_i += 1
            if eng.dma_cnt[s_i]:
                eng.eng.wait_ge(eng.dma_sems[s_i], eng.dma_cnt[s_i])
            instr = build()
            eng.dma_cnt[s_i] += 16
            instr.then_inc(eng.dma_sems[s_i], 16)
            tok = (eng.dma_sems[s_i], eng.dma_cnt[s_i])
        else:
            instr = build()
        if not dma:
            eng.n += 1
            instr.then_inc(eng.sem, 1)
            tok = (eng.sem, eng.n)
        for h in reads:
            self.buf(h).rtoks.append(tok)
        for h in writes:
            b = self.buf(h)
            b.wtok = tok
            b.rtoks = []
        return tok


def build_nc():
    nc = bass.Bass("TRN2", target_bir_lowering=False)
    TR = Tracker()

    x_h = nc.dram_tensor("x", [S, D], F32, kind="ExternalInput")
    wrT_h = nc.dram_tensor("wrT", [D, D], F32, kind="ExternalInput")
    wkT_h = nc.dram_tensor("wkT", [D, D], F32, kind="ExternalInput")
    wvT_h = nc.dram_tensor("wvT", [D, D], F32, kind="ExternalInput")
    woT_h = nc.dram_tensor("woT", [D, D], F32, kind="ExternalInput")
    wg_h = nc.dram_tensor("wg", [NF, D], F32, kind="ExternalInput")
    wu_h = nc.dram_tensor("wu", [NF, D], F32, kind="ExternalInput")
    wd_h = nc.dram_tensor("wd", [D, NF], F32, kind="ExternalInput")
    tmr_h = nc.dram_tensor("tmr", [128, NCh], F32, kind="ExternalInput")
    tmk_h = nc.dram_tensor("tmk", [128, NCh], F32, kind="ExternalInput")
    tmv_h = nc.dram_tensor("tmv", [128, NCh], F32, kind="ExternalInput")
    td_h = nc.dram_tensor("td", [128, NCh], F32, kind="ExternalInput")
    tf_h = nc.dram_tensor("tf", [128, NCh], F32, kind="ExternalInput")
    lng_h = nc.dram_tensor("lng", [128, NCh], F32, kind="ExternalInput")
    lnb_h = nc.dram_tensor("lnb", [128, NCh], F32, kind="ExternalInput")
    out_h = nc.dram_tensor("out", [S, D], F32, kind="ExternalOutput")

    ident_np = np.eye(128, dtype=bf16np)
    sel_np = np.zeros((128, 2), dtype=bf16np); sel_np[0:64, 0] = 1; sel_np[64:128, 1] = 1
    ident_d = nc.inline_tensor(np.asarray(ident_np).view(np.uint16), "ident_d")
    sel_d = nc.inline_tensor(np.asarray(sel_np).view(np.uint16), "sel_d")
    selT_d = nc.inline_tensor(np.ascontiguousarray(sel_np.T).view(np.uint16), "selT_d")

    xnbf_d = nc.dram_tensor("xnbf_d", [S, D], BF16, kind="Internal")
    xq_d = nc.dram_tensor("xq_d", [S, D], BF16, kind="Internal")
    xqT_d = nc.dram_tensor("xqT_d", [128, NCh * S], BF16, kind="Internal")
    wgq_d = nc.dram_tensor("wgq_d", [NF, D], BF16, kind="Internal")
    wuq_d = nc.dram_tensor("wuq_d", [NF, D], BF16, kind="Internal")
    wdq_d = nc.dram_tensor("wdq_d", [D, NF], BF16, kind="Internal")
    wdqT_d = nc.dram_tensor("wdqT_d", [128, NNC * D], BF16, kind="Internal")
    rwkv_d = nc.dram_tensor("rwkv_d", [D, S], BF16, kind="Internal")
    normed_d = nc.dram_tensor("normed_d", [D, S], BF16, kind="Internal")

    esg = contextlib.ExitStack()
    def mk_sb(stack):
        return lambda n, s_, d=F32: stack.enter_context(nc.sbuf_tensor(n, s_, d))
    sbg = mk_sb(esg)
    sem = lambda n: esg.enter_context(nc.semaphore(n))

    sv = sem("sv"); ss = sem("ss"); st_ = sem("st"); sg_ = sem("sg")
    NSL_DMA = 8
    d_sy = [sem(f"d_sy{i}") for i in range(NSL_DMA)]
    d_gp = [sem(f"d_gp{i}") for i in range(NSL_DMA)]
    d_sc = [sem(f"d_sc{i}") for i in range(NSL_DMA)]
    d_out = [sem(f"d_out{i}") for i in range(4)]
    V = Eng(nc, nc.vector, sv)
    SC = Eng(nc, nc.scalar, ss)
    T = Eng(nc, nc.tensor, st_)
    G = Eng(nc, nc.gpsimd, sg_)
    SY = Eng(nc, nc.sync, object(), dma_sems=d_sy)
    GQ = Eng(nc, nc.gpsimd, sg_, dma_sems=d_gp)
    SCQ = Eng(nc, nc.scalar, ss, dma_sems=d_sc)
    OQ = Eng(nc, nc.scalar, ss, dma_sems=d_out)
    GQ.watermark = G.watermark
    SCQ.watermark = SC.watermark
    OQ.watermark = SC.watermark
    TR.set_mutex((V, SC))

    TPOSE = object()
    vop = lambda b, r=(), w=(): TR.op(V, b, r, w)
    sop = lambda b, r=(), w=(): TR.op(SC, b, r, w)
    top = lambda b, r=(), w=(): TR.op(T, b, r, w)
    gop = lambda b, r=(), w=(): TR.op(G, b, r, w)
    dma_sy = lambda b, r=(), w=(): TR.op(SY, b, r, w, dma=True)
    dma_gp = lambda b, r=(), w=(): TR.op(GQ, b, r, w, dma=True)
    dma_sc = lambda b, r=(), w=(): TR.op(SCQ, b, r, w, dma=True)
    dma_out = lambda b, r=(), w=(): TR.op(OQ, b, r, w, dma=True)

    all_sems = [sv, ss, st_, sg_, d_sy, d_gp, d_out]

    def fence():
        vals = {id(sv): V.n, id(ss): SC.n, id(st_): T.n, id(sg_): G.n}
        for Q in (SY, GQ, SCQ, OQ):
            for s_, c_ in zip(Q.dma_sems, Q.dma_cnt):
                if c_:
                    nc.vector.wait_ge(s_, c_)
                    vals[id(s_)] = c_
        nc.all_engine_barrier()
        for E in (V, SC, T, G, SY):
            E.watermark.update(vals)

    ident = sbg("ident", [128, 128], BF16)
    selb = sbg("selb", [128, 2], BF16)
    selTb = sbg("selTb", [2, 128], BF16)
    tmr = sbg("tmr_s", [128, NCh]); tmk = sbg("tmk_s", [128, NCh]); tmv = sbg("tmv_s", [128, NCh])
    td = sbg("td_s", [128, NCh]); tf = sbg("tf_s", [128, NCh])
    lng = sbg("lng_s", [128, NCh]); lnb = sbg("lnb_s", [128, NCh])
    e1t = sbg("e1t", [128, NCh]); ewt = sbg("ewt", [128, NCh]); ewit = sbg("ewit", [128, NCh])
    eut = sbg("eut", [128, NCh]); eumw = sbg("eumw", [128, NCh])
    statS = sbg("statS", [128, 16]); statQ = sbg("statQ", [128, 16])
    scl128 = sbg("scl128", [128, NCh]); bia128 = sbg("bia128", [128, NCh])
    X0 = sbg("X0", [128, QW]); X1 = sbg("X1", [128, QW])
    X2 = sbg("X2", [128, QW]); X3 = sbg("X3", [128, QW])
    Qb1 = sbg("Qb1", [128, QW // 16]); Qb2 = sbg("Qb2", [128, QW // 16])
    Qout = sbg("Qout", [128, QW], BF16)
    Qout2 = sbg("Qout2", [128, QW], BF16)
    junkw = sbg("junkw", [128, QW])
    TR.op(V, lambda: nc.vector.memset(junkw[:, :], 0.0), (), (junkw,))

    def vguard(build_out_in, r=(), w=(), wd=64):
        return TR.op(V, build_out_in, r, w)
    vrecip = vguard


    PA = esg.enter_context(nc.psum_tensor("PA", [128, NHF], F32))
    PB = esg.enter_context(nc.psum_tensor("PB", [128, NHF], F32))
    PC = esg.enter_context(nc.psum_tensor("PC", [128, D], F32))

    for dst, src in ((ident, ident_d), (selb, sel_d), (selTb, selT_d)):
        dma_sy((lambda dst=dst, src=src: nc.sync.dma_start(dst[:, :], src.ap().bitcast(BF16))),
               w=[dst])
    for dst, src in ((tmr, tmr_h), (tmk, tmk_h), (tmv, tmv_h), (td, td_h),
                     (tf, tf_h), (lng, lng_h), (lnb, lnb_h)):
        dma_sy((lambda dst=dst, src=src: nc.sync.dma_start(dst[:, :], src.ap())), w=[dst])
    sop(lambda: nc.scalar.activation(e1t[:, :], td[:, :], AF.Exp), r=[td], w=[e1t])
    sop(lambda: nc.scalar.activation(ewt[:, :], e1t[:, :], AF.Exp, scale=-1.0), r=[e1t], w=[ewt])
    sop(lambda: nc.scalar.activation(ewit[:, :], e1t[:, :], AF.Exp, scale=1.0), r=[e1t], w=[ewit])
    sop(lambda: nc.scalar.activation(eut[:, :], tf[:, :], AF.Exp), r=[tf], w=[eut])
    vop(lambda: nc.vector.tensor_sub(eumw[:, :], eut[:, :], ewit[:, :]), r=[eut, ewit], w=[eumw])

    def expand16(dst, src, nblk):
        dw = dst.shape[1]
        for j in range(16):
            vop(lambda j=j: nc.vector.tensor_copy(
                   bass.AP(dst, j, [[dw, 128], [16, nblk]]), src[:, 0:nblk]),
               r=[src], w=[dst])

    def quant(src_ap, src_h, W_, out_ap, out_h):
        nblk = W_ // 16
        src3 = bass.AP(src_h, src_ap.offset, [[src_h.shape[1], 128], [16, nblk], [1, 16]])
        vop(lambda: nc.vector.tensor_reduce(Qb1[:, 0:nblk], src3, AX.X, OP.max,
                                            apply_absolute_value=True), r=[src_h], w=[Qb1])
        vop(lambda: nc.vector.tensor_scalar(Qb2[:, 0:nblk], Qb1[:, 0:nblk], 1.0 / 6.0, 1e-12,
                                            OP.mult, OP.max), r=[Qb1], w=[Qb2])
        vop(lambda: nc.vector.reciprocal(Qb1[:, 0:nblk], Qb2[:, 0:nblk]), r=[Qb2], w=[Qb1])
        expand16(X2, Qb1, nblk)
        vop(lambda: nc.vector.tensor_mul(X1[:, 0:W_], src_ap, X2[:, 0:W_]),
            r=[src_h, X2], w=[X1])
        vop(lambda: nc.vector.tensor_mul(X2[:, 0:W_], X1[:, 0:W_], X1[:, 0:W_]),
            r=[X1], w=[X2])
        MID2 = (0.0625, 0.5625, 1.5625, 3.0625, 6.25, 12.25, 25.0)
        DLT = (0.5, 0.5, 0.5, 0.5, 1.0, 1.0, 2.0)
        vop(lambda: nc.vector.tensor_scalar(X3[:, 0:W_], X2[:, 0:W_], MID2[0], DLT[0],
                                            OP.is_ge, OP.mult), r=[X2], w=[X3])
        for j in range(1, 7):
            vop(lambda j=j: nc.vector.tensor_scalar(junkw[:, 0:W_], X2[:, 0:W_], MID2[j],
                                                    DLT[j], OP.is_ge, OP.mult),
                r=[X2], w=[junkw])
            vop(lambda: nc.vector.tensor_add(X3[:, 0:W_], X3[:, 0:W_], junkw[:, 0:W_]),
                r=[X3, junkw], w=[X3])
        vop(lambda: nc.vector.tensor_scalar(X2[:, 0:W_], X1[:, 0:W_], 0.0, 2.0,
                                            OP.is_ge, OP.mult), r=[X1], w=[X2])
        vop(lambda: nc.vector.tensor_scalar_add(X2[:, 0:W_], X2[:, 0:W_], -1.0),
            r=[X2], w=[X2])
        vop(lambda: nc.vector.tensor_mul(X3[:, 0:W_], X3[:, 0:W_], X2[:, 0:W_]),
            r=[X3, X2], w=[X3])
        expand16(X2, Qb2, nblk)
        vop(lambda: nc.vector.tensor_mul(out_ap, X3[:, 0:W_], X2[:, 0:W_]),
            r=[X3, X2], w=[out_h])

    es1 = contextlib.ExitStack(); sb1 = mk_sb(es1)
    xa = sb1("xa", [128, D]); xb2 = sb1("xb2", [128, D])
    xnf = sb1("xnf", [128, D]); xnb = sb1("xnb", [128, D], BF16)
    lnScr = sb1("lnScr", [128, D])
    lnS = sb1("lnS", [128, 1]); lnQ = sb1("lnQ", [128, 1])
    lnT1 = sb1("lnT1", [128, 1]); lnT2 = sb1("lnT2", [128, 1])
    lnR = sb1("lnR", [128, 1]); lnB = sb1("lnB", [128, 1])
    wnat = [sb1("wnatA", [128, NF]), sb1("wnatB", [128, NF])]
    tstage = sb1("tstage", [128, S], BF16)

    xtiles = [xa, xb2]
    for i in range(NT):
        xt_ = xtiles[i % 2]
        dma_sy(lambda i=i, xt_=xt_: nc.sync.dma_start(xt_[:, :],
                                                      x_h.ap()[128 * i:128 * (i + 1), :]),
               w=[xt_])
        vguard(lambda xt_=xt_: nc.vector.reduce_sum(lnS[:, :], xt_[:, :], AX.X),
            r=[xt_], w=[lnS], wd=D)
        sop(lambda xt_=xt_: nc.scalar.activation(X3[:, 0:D], xt_[:, :], AF.Square), r=[xt_], w=[X3])
        vguard(lambda: nc.vector.reduce_sum(lnQ[:, :], X3[:, 0:D], AX.X), r=[X3], w=[lnQ], wd=D)
        vop(lambda: nc.vector.tensor_scalar_mul(lnT1[:, :], lnS[:, :], 1.0 / D), r=[lnS], w=[lnT1])
        vop(lambda: nc.vector.tensor_scalar_mul(lnT2[:, :], lnQ[:, :], 1.0 / D), r=[lnQ], w=[lnT2])
        vop(lambda: nc.vector.tensor_mul(lnS[:, :], lnT1[:, :], lnT1[:, :]), r=[lnT1], w=[lnS])
        vop(lambda: nc.vector.tensor_sub(lnT2[:, :], lnT2[:, :], lnS[:, :]), r=[lnT2, lnS], w=[lnT2])
        vop(lambda: nc.vector.tensor_scalar_add(lnT2[:, :], lnT2[:, :], 1e-5), r=[lnT2], w=[lnT2])
        sop(lambda: nc.scalar.activation(lnT2[:, :], lnT2[:, :], AF.Ln), r=[lnT2], w=[lnT2])
        sop(lambda: nc.scalar.activation(lnR[:, :], lnT2[:, :], AF.Exp, scale=-0.5),
            r=[lnT2], w=[lnR])
        vop(lambda: nc.vector.scalar_tensor_tensor(lnB[:, :], lnT1[:, :], -1.0, lnR[:, :],
                                                   OP.mult, OP.mult), r=[lnT1, lnR], w=[lnB])
        vop(lambda xt_=xt_: nc.vector.tensor_scalar(xnf[:, :], xt_[:, :], lnR[:, :],
                                                    lnB[:, :], OP.mult, OP.add),
            r=[xt_, lnR, lnB], w=[xnf])
        sop(lambda: nc.scalar.copy(xnb[:, :], xnf[:, :]), r=[xnf], w=[xnb])
        dma_sc(lambda i=i: nc.scalar.dma_start(xnbf_d.ap()[128 * i:128 * (i + 1), :],
                                               xnb[:, :]), r=[xnb], w=[xnbf_d])
        quant(xnf[:, 0:D], xnf, D, Qout[:, 0:D], Qout)
        dma_sc(lambda i=i: nc.scalar.dma_start(xq_d.ap()[128 * i:128 * (i + 1), :],
                                               Qout[:, 0:D]), r=[Qout], w=[xq_d])

    fence()
    for j in range(NCh):
        dma_sy(lambda j=j: nc.sync.dma_start(tstage[:, :], xq_d.ap()[:, 128 * j:128 * (j + 1)],
                                             transpose=True), r=[xq_d, TPOSE], w=[tstage, TPOSE])
        dma_sc(lambda j=j: nc.scalar.dma_start(xqT_d.ap()[:, S * j:S * (j + 1)], tstage[:, :]),
               r=[tstage], w=[xqT_d])
    fence()
    fence()
    for w_h, wq_dr, nrows, rowW in ((wg_h, wgq_d, NF, D), (wu_h, wuq_d, NF, D),
                                    (wd_h, wdq_d, D, NF)):
        for c in range(nrows // 128):
            wn = wnat[c % 2]
            dma_sy(lambda w_h=w_h, c=c, wn=wn, rowW=rowW:
                   nc.sync.dma_start(wn[:, 0:rowW], w_h.ap()[128 * c:128 * (c + 1), :]), w=[wn])
            nh = 1 if rowW <= QW else 2
            for hh in range(nh):
                w0 = (rowW // nh) * hh
                ww = rowW // nh
                quant(wn[:, w0:w0 + ww], wn, ww, Qout[:, 0:ww], Qout)
                dma_sc(lambda wq_dr=wq_dr, c=c, w0=w0, ww=ww:
                       nc.scalar.dma_start(wq_dr.ap()[128 * c:128 * (c + 1), w0:w0 + ww],
                                           Qout[:, 0:ww]), r=[Qout], w=[wq_dr])

    es1.close()

    es23 = contextlib.ExitStack(); sb23 = mk_sb(es23)
    xrT = [sb23(f"xrT{j}", [128, S], BF16) for j in range(NCh)]
    xkT = [sb23(f"xkT{j}", [128, S], BF16) for j in range(NCh)]
    xvT = [sb23(f"xvT{j}", [128, S], BF16) for j in range(NCh)]
    es2 = contextlib.ExitStack(); sb2 = mk_sb(es2)
    OFF = 16
    xnT = [sb2(f"xnT{j}", [128, S + OFF], BF16) for j in range(NCh)]
    dmix = sb2("dmix", [128, S], BF16)
    for j in range(NCh):
        xj = xnT[j]
        vop(lambda xj=xj: nc.vector.memset(xj[:, OFF - 1:OFF], 0.0), w=[xj])
        dma_sy(lambda j=j, xj=xj: nc.sync.dma_start(
                   xj[:, OFF:S + OFF], xnbf_d.ap()[:, 128 * j:128 * (j + 1)], transpose=True),
               r=[xnbf_d, TPOSE], w=[xj, TPOSE])
        vop(lambda xj=xj: nc.vector.tensor_sub(dmix[:, :], xj[:, OFF:S + OFF],
                                               xj[:, OFF - 1:S + OFF - 1]), r=[xj], w=[dmix])
        for lam, dst in ((tmr, xrT[j]), (tmk, xkT[j]), (tmv, xvT[j])):
            vop(lambda lam=lam, dst=dst, xj=xj, j=j:
                nc.vector.scalar_tensor_tensor(dst[:, :], dmix[:, :], lam[:, j:j + 1],
                                               xj[:, OFF - 1:S + OFF - 1], OP.mult, OP.add),
                r=[dmix, lam, xj], w=[dst])
    fence()
    es2.close()

    es3 = contextlib.ExitStack(); sb3 = mk_sb(es3)
    wsl = {w: [sb3(f"wsl_{w}{b}", [128, D], BF16) for b in range(2)] for w in "rkv"}
    wslf = {w: [sb3(f"wslf_{w}{b}", [128, D]) for b in range(2)] for w in "rkv"}
    wsrc = {"r": wrT_h, "k": wkT_h, "v": wvT_h}
    sgr = sb3("sgr", [128, HALF])
    f_t = sb3("f_t", [128, HALF])
    ewfull = sb3("ewfull", [128, HALF])
    rwkvf = sb3("rwkvf", [128, HALF])
    rwkvb = sb3("rwkvb", [128, HALF], BF16)
    B0t = sb3("B0t", [128, HALF + 1]); Bt = sb3("Bt", [128, HALF + 1])
    gt = sb3("gt", [128, HALF + 1]); At = sb3("At", [128, HALF + 1])
    cB0 = sb3("cB0", [128, 1]); cB = sb3("cB", [128, 1]); cA = sb3("cA", [128, 1])
    ekt = lambda: X0[:, 0:HALF]
    ekvt = lambda: X1[:, 0:HALF]
    sc1 = lambda: X2[:, 0:HALF]
    sc2 = lambda: X3[:, 0:HALF]
    r_ps = lambda: PA.ap()[:, 0:HALF]
    k_ps = lambda: PB.ap()[:, 0:HALF]
    v_ps = lambda: PC.ap()[:, 0:HALF]

    for c in range(NCh):
        vop(lambda: nc.vector.memset(ewfull[:, :], 1.0), w=[ewfull])
        vop(lambda c=c: nc.vector.tensor_scalar_mul(ewfull[:, :], ewfull[:, :], ewt[:, c:c + 1]),
            r=[ewfull, ewt], w=[ewfull])
        for wname in "rkv":
            wb = wsl[wname][c % 2]
            wf_ = wslf[wname][c % 2]
            dma_sy(lambda wname=wname, c=c, wf_=wf_: nc.sync.dma_start(
                       wf_.ap().rearrange("p (f q) -> p f q", f=NCh),
                       wsrc[wname].ap().rearrange("(f p) q -> p f q", p=128)
                       [:, :, 128 * c:128 * (c + 1)]), w=[wf_])
            sop(lambda wb=wb, wf_=wf_: nc.scalar.copy(wb[:, :], wf_[:, :]), r=[wf_], w=[wb])
        for t_ in (cB0, cB, cA):
            vop(lambda t_=t_: nc.vector.memset(t_[:, :], 0.0), w=[t_])
        vop(lambda: nc.vector.memset(B0t[:, 0:1], 0.0), w=[B0t])
        vop(lambda: nc.vector.memset(Bt[:, 0:1], 0.0), w=[Bt])
        vop(lambda: nc.vector.memset(gt[:, 0:1], 1.0), w=[gt])

        for th in range(2):
            t0 = th * HALF
            for wname, pst, psum_t, xsrc in (("r", r_ps, PA, xrT), ("k", k_ps, PB, xkT),
                                             ("v", v_ps, PC, xvT)):
                wb = wsl[wname][c % 2]
                for fj in range(NCh):
                    for nsl in range(2):
                        top(lambda wb=wb, fj=fj, nsl=nsl, pst=pst, xsrc=xsrc, t0=t0:
                            nc.tensor.matmul(pst()[:, 512 * nsl:512 * (nsl + 1)],
                                             wb.ap().rearrange("p (f q) -> p f q", f=NCh)[:, fj, :],
                                             xsrc[fj][:, t0 + 512 * nsl:t0 + 512 * (nsl + 1)],
                                             start=(fj == 0), stop=(fj == NCh - 1)),
                            r=[wb, xsrc[fj]], w=[psum_t])
            sop(lambda: nc.scalar.activation(ekt(), k_ps(), AF.Exp), r=[PB], w=[X0])
            vop(lambda: nc.vector.tensor_mul(ekvt(), ekt(), v_ps()), r=[X0, PC], w=[X1])
            sop(lambda: nc.scalar.activation(sgr[:, :], r_ps(), AF.Exp, scale=-1.0),
                r=[PA], w=[sgr])
            vop(lambda: nc.vector.tensor_scalar_add(sgr[:, :], sgr[:, :], 1.0), r=[sgr], w=[sgr])
            vrecip(lambda: nc.vector.reciprocal(sgr[:, :], sgr[:, :]), r=[sgr], w=[sgr], wd=HALF)
            vop(lambda: nc.vector.tensor_tensor_scan(B0t[:, 1:], ewfull[:, :], ekt(),
                                                     cB0[:, :], OP.mult, OP.add),
                r=[ewfull, X0, cB0], w=[B0t])
            vop(lambda: nc.vector.tensor_copy(cB0[:, :], B0t[:, HALF:]), r=[B0t], w=[cB0])
            vop(lambda c=c: nc.vector.tensor_scalar_mul(sc1(), B0t[:, 0:HALF], ewt[:, c:c + 1]),
                r=[B0t, ewt], w=[X2])
            vop(lambda: nc.vector.tensor_max(sc1(), sc1(), ekt()), r=[X2, X0], w=[X2])
            vrecip(lambda: nc.vector.reciprocal(sc2(), B0t[:, 1:]), r=[B0t], w=[X3], wd=HALF)
            vop(lambda: nc.vector.tensor_mul(sc1(), sc1(), sc2()), r=[X2, X3], w=[X2])
            vop(lambda: nc.vector.tensor_scalar(f_t[:, :], sc1(), EPS, 1.0, OP.mult, OP.add),
                r=[X2], w=[f_t])
            vop(lambda c=c: nc.vector.tensor_scalar_mul(sc1(), f_t[:, :], ewt[:, c:c + 1]),
                r=[f_t, ewt], w=[X2])
            vop(lambda: nc.vector.tensor_mul(sc2(), f_t[:, :], ekt()), r=[f_t, X0], w=[X3])
            vop(lambda: nc.vector.tensor_tensor_scan(Bt[:, 1:], sc1(), sc2(), cB[:, :],
                                                     OP.mult, OP.add), r=[X2, X3, cB], w=[Bt])
            vop(lambda: nc.vector.tensor_copy(cB[:, :], Bt[:, HALF:]), r=[Bt], w=[cB])
            vop(lambda c=c: nc.vector.tensor_scalar_mul(sc1(), Bt[:, 0:HALF], ewt[:, c:c + 1]),
                r=[Bt, ewt], w=[X2])
            vop(lambda: nc.vector.tensor_max(sc1(), sc1(), ekt()), r=[X2, X0], w=[X2])
            vrecip(lambda: nc.vector.reciprocal(sc1(), sc1()), r=[X2], w=[X2], wd=HALF)
            vop(lambda: nc.vector.tensor_mul(gt[:, 1:], Bt[:, 1:], sc1()), r=[Bt, X2], w=[gt])
            vop(lambda c=c: nc.vector.tensor_scalar_mul(sc1(), gt[:, 0:HALF], ewt[:, c:c + 1]),
                r=[gt, ewt], w=[X2])
            vop(lambda: nc.vector.tensor_tensor_scan(At[:, 1:], sc1(), ekvt(), cA[:, :],
                                                     OP.mult, OP.add), r=[X2, X1, cA], w=[At])
            vop(lambda: nc.vector.tensor_copy(cA[:, :], At[:, HALF:]), r=[At], w=[cA])
            vop(lambda: nc.vector.tensor_copy(gt[:, 0:1], gt[:, HALF:]), r=[gt], w=[gt])
            vop(lambda c=c: nc.vector.tensor_scalar_mul(sc1(), At[:, 1:], ewit[:, c:c + 1]),
                r=[At, ewit], w=[X2])
            vop(lambda c=c: nc.vector.scalar_tensor_tensor(sc1(), ekvt(), eumw[:, c:c + 1],
                                                           sc1(), OP.mult, OP.add),
                r=[X1, eumw, X2], w=[X2])
            vop(lambda c=c: nc.vector.tensor_scalar_mul(ekt(), ekt(), eut[:, c:c + 1]),
                r=[X0, eut], w=[X0])
            vop(lambda: nc.vector.tensor_max(sc2(), Bt[:, 0:HALF], ekt()), r=[Bt, X0], w=[X3])
            vop(lambda: nc.vector.tensor_add(ekt(), Bt[:, 0:HALF], ekt()), r=[Bt, X0], w=[X0])
            vop(lambda: nc.vector.scalar_tensor_tensor(sc2(), sc2(), EPS, ekt(),
                                                       OP.mult, OP.add), r=[X3, X0], w=[X3])
            vrecip(lambda: nc.vector.reciprocal(sc2(), sc2()), r=[X3], w=[X3], wd=HALF)
            vop(lambda: nc.vector.tensor_mul(sc1(), sc1(), sc2()), r=[X2, X3], w=[X2])
            col = 2 * c + th
            vop(lambda: nc.vector.tensor_mul(rwkvf[:, :], sc1(), sgr[:, :]),
                r=[X2, sgr], w=[rwkvf])
            vguard(lambda col=col: nc.vector.reduce_sum(statS[:, col:col + 1], rwkvf[:, :], AX.X),
                r=[rwkvf], w=[statS], wd=HALF)
            sop(lambda: nc.scalar.activation(ekvt(), rwkvf[:, :], AF.Square), r=[rwkvf], w=[X1])
            vguard(lambda col=col: nc.vector.reduce_sum(statQ[:, col:col + 1], ekvt(), AX.X),
                r=[X1], w=[statQ], wd=HALF)
            sop(lambda: nc.scalar.copy(rwkvb[:, :], rwkvf[:, :]), r=[rwkvf], w=[rwkvb])
            dma_sc(lambda c=c, t0=t0: nc.scalar.dma_start(
                       rwkv_d.ap()[128 * c:128 * (c + 1), t0:t0 + HALF], rwkvb[:, :]),
                   r=[rwkvb], w=[rwkv_d])
            vop(lambda: nc.vector.tensor_copy(B0t[:, 0:1], cB0[:, :]), r=[cB0], w=[B0t])
            vop(lambda: nc.vector.tensor_copy(Bt[:, 0:1], cB[:, :]), r=[cB], w=[Bt])
    fence()
    es3.close()
    es23.close()

    es4 = contextlib.ExitStack(); sb4 = mk_sb(es4)
    gs8 = sb4("gs8", [128, NCh]); gq8 = sb4("gq8", [128, NCh])
    hi_b = sb4("hi_b", [128, 2 * NCh], BF16); lo_b = sb4("lo_b", [128, 2 * NCh], BF16)
    lo_f = sb4("lo_f", [128, 2 * NCh])
    tot2 = sb4("tot2", [2, 2 * NCh]); tot2b = sb4("tot2b", [2, 2 * NCh], BF16)
    sall = sb4("sall", [128, NCh]); qall = sb4("qall", [128, NCh])
    gtmp1 = sb4("gtmp1", [128, NCh]); gtmp2 = sb4("gtmp2", [128, NCh])
    rload = sb4("rload", [128, S], BF16)
    nstage = sb4("nstage", [128, S], BF16)

    s_str = lambda tns, off: bass.AP(tns, off, [[16, 128], [2, NCh]])
    vop(lambda: nc.vector.tensor_add(gs8[:, :], s_str(statS, 0), s_str(statS, 1)),
        r=[statS], w=[gs8])
    vop(lambda: nc.vector.tensor_add(gq8[:, :], s_str(statQ, 0), s_str(statQ, 1)),
        r=[statQ], w=[gq8])
    vop(lambda: nc.vector.tensor_copy(hi_b[:, 0:NCh], gs8[:, :]), r=[gs8], w=[hi_b])
    vop(lambda: nc.vector.tensor_copy(hi_b[:, NCh:], gq8[:, :]), r=[gq8], w=[hi_b])
    vop(lambda: nc.vector.tensor_sub(lo_f[:, 0:NCh], gs8[:, :], hi_b[:, 0:NCh]),
        r=[gs8, hi_b], w=[lo_f])
    vop(lambda: nc.vector.tensor_sub(lo_f[:, NCh:], gq8[:, :], hi_b[:, NCh:]),
        r=[gq8, hi_b], w=[lo_f])
    vop(lambda: nc.vector.tensor_copy(lo_b[:, :], lo_f[:, :]), r=[lo_f], w=[lo_b])
    top(lambda: nc.tensor.matmul(PC.ap()[0:2, 0:2 * NCh], selb[:, :], hi_b[:, :],
                                 start=True, stop=False), r=[selb, hi_b], w=[PC])
    top(lambda: nc.tensor.matmul(PC.ap()[0:2, 0:2 * NCh], selb[:, :], lo_b[:, :],
                                 start=False, stop=True), r=[selb, lo_b], w=[PC])
    vop(lambda: nc.vector.tensor_copy(tot2[:, :], PC.ap()[0:2, 0:2 * NCh]), r=[PC], w=[tot2])
    vop(lambda: nc.vector.tensor_copy(tot2b[:, :], tot2[:, :]), r=[tot2], w=[tot2b])
    top(lambda: nc.tensor.matmul(PC.ap()[:, 0:2 * NCh], selTb[:, :], tot2b[:, :],
                                 start=True, stop=True), r=[selTb, tot2b], w=[PC])
    vop(lambda: nc.vector.tensor_copy(sall[:, :], PC.ap()[:, 0:NCh]), r=[PC], w=[sall])
    vop(lambda: nc.vector.tensor_copy(qall[:, :], PC.ap()[:, NCh:2 * NCh]), r=[PC], w=[qall])
    NEL = float(S * HD)
    vop(lambda: nc.vector.tensor_scalar_mul(sall[:, :], sall[:, :], 1.0 / NEL), r=[sall], w=[sall])
    vop(lambda: nc.vector.tensor_scalar_mul(qall[:, :], qall[:, :], 1.0 / NEL), r=[qall], w=[qall])
    vop(lambda: nc.vector.tensor_mul(gtmp1[:, :], sall[:, :], sall[:, :]), r=[sall], w=[gtmp1])
    vop(lambda: nc.vector.tensor_sub(qall[:, :], qall[:, :], gtmp1[:, :]),
        r=[qall, gtmp1], w=[qall])
    vop(lambda: nc.vector.tensor_scalar_add(qall[:, :], qall[:, :], 1e-5), r=[qall], w=[qall])
    sop(lambda: nc.scalar.activation(qall[:, :], qall[:, :], AF.Ln), r=[qall], w=[qall])
    sop(lambda: nc.scalar.activation(gtmp2[:, :], qall[:, :], AF.Exp, scale=-0.5),
        r=[qall], w=[gtmp2])
    vop(lambda: nc.vector.tensor_mul(scl128[:, :], gtmp2[:, :], lng[:, :]),
        r=[gtmp2, lng], w=[scl128])
    vop(lambda: nc.vector.tensor_mul(gtmp1[:, :], sall[:, :], scl128[:, :]),
        r=[sall, scl128], w=[gtmp1])
    vop(lambda: nc.vector.tensor_sub(bia128[:, :], lnb[:, :], gtmp1[:, :]),
        r=[lnb, gtmp1], w=[bia128])
    for c in range(NCh):
        dma_sy(lambda c=c: nc.sync.dma_start(rload[:, :], rwkv_d.ap()[128 * c:128 * (c + 1), :]),
               r=[rwkv_d], w=[rload])
        vop(lambda c=c: nc.vector.tensor_scalar(nstage[:, :], rload[:, :], scl128[:, c:c + 1],
                                                bia128[:, c:c + 1], OP.mult, OP.add),
            r=[rload, scl128, bia128], w=[nstage])
        dma_sc(lambda c=c: nc.scalar.dma_start(normed_d.ap()[128 * c:128 * (c + 1), :],
                                               nstage[:, :]), r=[nstage], w=[normed_d])
    fence()
    es4.close()

    es5 = contextlib.ExitStack(); sb5 = mk_sb(es5)
    wgT = [sb5(f"wgT{j}", [128, NF], BF16) for j in range(NCh)]
    wuT = [sb5(f"wuT{j}", [128, NF], BF16) for j in range(NCh)]
    woT = [sb5(f"woT{j}", [128, D], BF16) for j in range(NCh)]
    wdT_st = sb5("wdT_st", [128, D], BF16)
    wdq_sl = sb5("wdq_sl", [128, (NNC // 2) * D], BF16)
    xqT_sl = [sb5("xqT_A", [128, NCh, 128], BF16), sb5("xqT_B", [128, NCh, 128], BF16)]
    nrm_sl = [sb5("nrm_A", [128, NCh, 128], BF16), sb5("nrm_B", [128, NCh, 128], BF16)]
    x_sl = [sb5("x_slA", [128, D]), sb5("x_slB", [128, D])]

    for j in range(NCh):
        dma_sy(lambda j=j: nc.sync.dma_start(wgT[j][:, :], wgq_d.ap()[:, 128 * j:128 * (j + 1)],
                                             transpose=True), r=[wgq_d, TPOSE], w=[wgT[j], TPOSE])
        dma_sy(lambda j=j: nc.sync.dma_start(wuT[j][:, :], wuq_d.ap()[:, 128 * j:128 * (j + 1)],
                                             transpose=True), r=[wuq_d, TPOSE], w=[wuT[j], TPOSE])
    for nj in range(NNC):
        dma_sy(lambda nj=nj: nc.sync.dma_start(wdT_st[:, :],
                                               wdq_d.ap()[:, 128 * nj:128 * (nj + 1)],
                                               transpose=True), r=[wdq_d, TPOSE], w=[wdT_st, TPOSE])
        dma_sc(lambda nj=nj: nc.scalar.dma_start(wdqT_d.ap()[:, D * nj:D * (nj + 1)],
                                                 wdT_st[:, :]), r=[wdT_st], w=[wdqT_d])
    fence()
    woTf = sb5("woTf", [128, D])
    for j in range(NCh):
        dma_sy(lambda j=j: nc.sync.dma_start(woTf[:, :],
                                             woT_h.ap()[128 * j:128 * (j + 1), :]), w=[woTf])
        sop(lambda j=j: nc.scalar.copy(woT[j][:, :], woTf[:, :]), r=[woTf], w=[woT[j]])

    gate_ps = lambda: PA.ap()[:, 0:NHF]
    up_ps = lambda: PB.ap()[:, 0:NHF]
    down_ps = lambda: PC.ap()
    hqT_ps = lambda: PA.ap().bitcast(BF16)[:, 0:NHF]
    s_ps = lambda: PB.ap()[:, 0:D]
    NSL = ((0, 512), (512, 1024), (1024, NHF))

    for i in range(NT):
        t0 = 128 * i
        xq_b = xqT_sl[i % 2]
        nr_b = nrm_sl[i % 2]
        xx = x_sl[i % 2]
        dma_sy(lambda i=i, xq_b=xq_b: nc.sync.dma_start(
                   xq_b[:, :, :],
                   xqT_d.ap().rearrange("p (a t) -> p a t", a=NCh)[:, :, t0:t0 + 128]),
               r=[xqT_d], w=[xq_b])
        dma_sy(lambda i=i, nr_b=nr_b: nc.sync.dma_start(
                   nr_b[:, :, :],
                   normed_d.ap().rearrange("(a p) t -> p a t", p=128)[:, :, t0:t0 + 128]),
               r=[normed_d], w=[nr_b])
        dma_sy(lambda i=i, xx=xx: nc.sync.dma_start(xx[:, :], x_h.ap()[t0:t0 + 128, :]),
               w=[xx])
        for hh in range(2):
            n0 = NHF * hh
            dma_sy(lambda hh=hh: nc.sync.dma_start(
                       wdq_sl[:, :], wdqT_d.ap()[:, (NNC // 2) * D * hh:(NNC // 2) * D * (hh + 1)]),
                   r=[wdqT_d], w=[wdq_sl])
            for fj in range(NCh):
                for (a, b) in NSL:
                    top(lambda fj=fj, a=a, b=b, xq_b=xq_b, n0=n0:
                        nc.tensor.matmul(gate_ps()[:, a:b], xq_b[:, fj, :],
                                         wgT[fj][:, n0 + a:n0 + b],
                                         start=(fj == 0), stop=(fj == NCh - 1)),
                        r=[xq_b, wgT[fj]], w=[PA])
                for (a, b) in NSL:
                    top(lambda fj=fj, a=a, b=b, xq_b=xq_b, n0=n0:
                        nc.tensor.matmul(up_ps()[:, a:b], xq_b[:, fj, :],
                                         wuT[fj][:, n0 + a:n0 + b],
                                         start=(fj == 0), stop=(fj == NCh - 1)),
                        r=[xq_b, wuT[fj]], w=[PB])
            sop(lambda: nc.scalar.activation(X2[:, 0:NHF], gate_ps(), AF.Silu), r=[PA], w=[X2])
            vop(lambda: nc.vector.tensor_mul(X0[:, 0:NHF], X2[:, 0:NHF], up_ps()),
                r=[X2, PB], w=[X0])
            quant(X0[:, 0:NHF], X0, NHF, Qout[:, 0:NHF], Qout)
            for pj in range(NHF // 128):
                top(lambda pj=pj: nc.tensor.transpose(
                        hqT_ps()[:, 128 * pj:128 * (pj + 1)], Qout[:, 128 * pj:128 * (pj + 1)],
                        ident[:, :]), r=[Qout, ident], w=[PA])
            sop(lambda: nc.scalar.copy(Qout2[:, 0:NHF], hqT_ps()), r=[PA], w=[Qout2])
            for pj in range(NHF // 128):
                nj = hh * (NHF // 128) + pj
                for (a, b) in ((0, 512), (512, 1024)):
                    top(lambda pj=pj, nj=nj, a=a, b=b:
                        nc.tensor.matmul(down_ps()[:, a:b], Qout2[:, 128 * pj:128 * (pj + 1)],
                                         wdq_sl.ap().rearrange("p (n q) -> p n q", n=NNC // 2)
                                         [:, pj, a:b],
                                         start=(nj == 0), stop=(nj == NNC - 1)),
                        r=[Qout2, wdq_sl], w=[PC])
        sop(lambda: nc.scalar.activation(X1[:, 0:D], down_ps(), AF.Silu), r=[PC], w=[X1])
        for cj in range(NCh):
            for (a, b) in ((0, 512), (512, 1024)):
                top(lambda cj=cj, a=a, b=b, nr_b=nr_b:
                    nc.tensor.matmul(s_ps()[:, a:b], nr_b[:, cj, :], woT[cj][:, a:b],
                                     start=(cj == 0), stop=(cj == NCh - 1)),
                    r=[nr_b, woT[cj]], w=[PB])
        vop(lambda: nc.vector.tensor_mul(X3[:, 0:D], X1[:, 0:D], s_ps()), r=[X1, PB], w=[X3])
        vop(lambda xx=xx: nc.vector.tensor_add(X3[:, 0:D], X3[:, 0:D], xx[:, :]),
            r=[X3, xx], w=[X3])
        dma_out(lambda i=i: nc.scalar.dma_start(out_h.ap()[t0:t0 + 128, :], X3[:, 0:D]),
                r=[X3])

    for Q in (SCQ, OQ):
        for s_, c_ in zip(Q.dma_sems, Q.dma_cnt):
            if c_:
                nc.sync.wait_ge(s_, c_)
    es5.close()
    esg.close()
    return nc


_cached = {}

def kernel(L0_attn_x, L0_attn_time_mix_r, L0_attn_time_mix_k, L0_attn_time_mix_v,
           L0_attn_W_r, L0_attn_W_k, L0_attn_W_v, L0_attn_W_o,
           L0_attn_time_decay, L0_attn_time_first, L0_attn_ln_gamma, L0_attn_ln_beta,
           L0_ffn_w_gate, L0_ffn_w_up, L0_ffn_w_down):
    x = np.ascontiguousarray(np.asarray(L0_attn_x, dtype=np.float32))
    B = x.shape[0]
    f32c = lambda a: np.ascontiguousarray(np.asarray(a, dtype=np.float32))
    tT = lambda a: np.ascontiguousarray(np.asarray(a, dtype=np.float32).T)
    pp = lambda a: np.ascontiguousarray(
        np.asarray(a, dtype=np.float32).reshape(-1).reshape(NCh, 128).T)

    shared = {
        "wrT": tT(L0_attn_W_r), "wkT": tT(L0_attn_W_k), "wvT": tT(L0_attn_W_v),
        "woT": tT(L0_attn_W_o),
        "wg": f32c(L0_ffn_w_gate), "wu": f32c(L0_ffn_w_up), "wd": f32c(L0_ffn_w_down),
        "tmr": pp(L0_attn_time_mix_r), "tmk": pp(L0_attn_time_mix_k),
        "tmv": pp(L0_attn_time_mix_v),
        "td": pp(L0_attn_time_decay), "tf": pp(L0_attn_time_first),
        "lng": pp(L0_attn_ln_gamma), "lnb": pp(L0_attn_ln_beta),
    }
    if "nc" not in _cached:
        _cached["nc"] = build_nc()
    nc = _cached["nc"]
    in_maps = [dict(shared, x=np.ascontiguousarray(x[b])) for b in range(B)]
    res = run_bass_kernel_spmd(nc, in_maps, core_ids=list(range(B)))
    return np.stack([r["out"] for r in res.results]).astype(np.float32)




# version 04
# speedup vs baseline: 1.0574x; 1.0142x over previous
import contextlib
import numpy as np
import ml_dtypes
import concourse.bass as bass
import concourse.mybir as mybir
from concourse.bass_utils import run_bass_kernel_spmd

F32 = mybir.dt.float32
BF16 = mybir.dt.bfloat16
I32 = mybir.dt.int32
AF = mybir.ActivationFunctionType
OP = mybir.AluOpType
AX = mybir.AxisListType

S = 2048
D = 1024
H = 16
HD = 64
NF = 2816
NT = S // 128
NCh = D // 128
NNC = NF // 128
HALF = 1024
NHF = NF // 2
QW = NHF
EPS = 1e-8
bf16np = ml_dtypes.bfloat16


class Eng:
    def __init__(self, nc, eng, sem, dma_sems=None):
        self.nc, self.eng, self.sem = nc, eng, sem
        self.n = 0
        self.dma_sems = dma_sems or []
        self.dma_cnt = [0] * len(self.dma_sems)
        self.dma_i = 0
        self.watermark = {}

    def wait_tok(self, tok):
        sem, val = tok
        key = id(sem)
        if self.watermark.get(key, -1) >= val:
            return
        self.eng.wait_ge(sem, val)
        self.watermark[key] = val


class Buf:
    __slots__ = ("wtok", "rtoks")
    def __init__(self):
        self.wtok = None
        self.rtoks = []


class Tracker:
    def __init__(self):
        self.bufs = {}
        self.drainers = {}

    def add_drainer(self, eng, junk_fn):
        self.drainers[id(eng.sem)] = [eng, junk_fn, -1, -1]

    def launder(self, consumer, tok):
        sem, val = tok
        if sem is consumer.sem:
            return tok
        d = self.drainers.get(id(sem))
        if d is None:
            return tok
        eng, junk_fn, covered, junk_val = d
        if covered >= val:
            return (sem, junk_val)
        pre = eng.n
        n0 = eng.n
        junk_fn()
        junk_fn()
        d[2] = pre
        d[3] = eng.n
        assert eng.n > n0
        return (sem, eng.n)

    def buf(self, h):
        k = id(h)
        if k not in self.bufs:
            self.bufs[k] = Buf()
        return self.bufs[k]

    def set_mutex(self, *pairs):
        self.mutex = pairs

    def op(self, eng, build, reads=(), writes=(), dma=False):
        for a, b in getattr(self, "mutex", ()):
            if eng is a and b.n:
                eng.wait_tok((b.sem, b.n))
            elif eng is b and a.n:
                eng.wait_tok((a.sem, a.n))
        toks = []
        for h in reads:
            b = self.buf(h)
            if b.wtok is not None:
                toks.append(b.wtok)
        for h in writes:
            b = self.buf(h)
            if b.wtok is not None:
                toks.append(b.wtok)
            toks.extend(b.rtoks)
        for t in toks:
            eng.wait_tok(self.launder(eng, t))
        if dma:
            s_i = eng.dma_i % len(eng.dma_sems)
            eng.dma_i += 1
            if eng.dma_cnt[s_i]:
                eng.eng.wait_ge(eng.dma_sems[s_i], eng.dma_cnt[s_i])
            instr = build()
            eng.dma_cnt[s_i] += 16
            instr.then_inc(eng.dma_sems[s_i], 16)
            tok = (eng.dma_sems[s_i], eng.dma_cnt[s_i])
        else:
            instr = build()
        if not dma:
            eng.n += 1
            instr.then_inc(eng.sem, 1)
            tok = (eng.sem, eng.n)
        for h in reads:
            self.buf(h).rtoks.append(tok)
        for h in writes:
            b = self.buf(h)
            b.wtok = tok
            b.rtoks = []
        return tok


def build_nc():
    nc = bass.Bass("TRN2", target_bir_lowering=False)
    TR = Tracker()

    x_h = nc.dram_tensor("x", [S, D], F32, kind="ExternalInput")
    wrT_h = nc.dram_tensor("wrT", [D, D], F32, kind="ExternalInput")
    wkT_h = nc.dram_tensor("wkT", [D, D], F32, kind="ExternalInput")
    wvT_h = nc.dram_tensor("wvT", [D, D], F32, kind="ExternalInput")
    woT_h = nc.dram_tensor("woT", [D, D], F32, kind="ExternalInput")
    wg_h = nc.dram_tensor("wg", [NF, D], F32, kind="ExternalInput")
    wu_h = nc.dram_tensor("wu", [NF, D], F32, kind="ExternalInput")
    wd_h = nc.dram_tensor("wd", [D, NF], F32, kind="ExternalInput")
    tmr_h = nc.dram_tensor("tmr", [128, NCh], F32, kind="ExternalInput")
    tmk_h = nc.dram_tensor("tmk", [128, NCh], F32, kind="ExternalInput")
    tmv_h = nc.dram_tensor("tmv", [128, NCh], F32, kind="ExternalInput")
    td_h = nc.dram_tensor("td", [128, NCh], F32, kind="ExternalInput")
    tf_h = nc.dram_tensor("tf", [128, NCh], F32, kind="ExternalInput")
    lng_h = nc.dram_tensor("lng", [128, NCh], F32, kind="ExternalInput")
    lnb_h = nc.dram_tensor("lnb", [128, NCh], F32, kind="ExternalInput")
    out_h = nc.dram_tensor("out", [S, D], F32, kind="ExternalOutput")

    ident_np = np.eye(128, dtype=bf16np)
    sel_np = np.zeros((128, 2), dtype=bf16np); sel_np[0:64, 0] = 1; sel_np[64:128, 1] = 1
    ident_d = nc.inline_tensor(np.asarray(ident_np).view(np.uint16), "ident_d")
    sel_d = nc.inline_tensor(np.asarray(sel_np).view(np.uint16), "sel_d")
    selT_d = nc.inline_tensor(np.ascontiguousarray(sel_np.T).view(np.uint16), "selT_d")

    xnbf_d = nc.dram_tensor("xnbf_d", [S, D], BF16, kind="Internal")
    xq_d = nc.dram_tensor("xq_d", [S, D], BF16, kind="Internal")
    xqT_d = nc.dram_tensor("xqT_d", [128, NCh * S], BF16, kind="Internal")
    wgq_d = nc.dram_tensor("wgq_d", [NF, D], BF16, kind="Internal")
    wuq_d = nc.dram_tensor("wuq_d", [NF, D], BF16, kind="Internal")
    wdq_d = nc.dram_tensor("wdq_d", [D, NF], BF16, kind="Internal")
    wdqT_d = nc.dram_tensor("wdqT_d", [128, NNC * D], BF16, kind="Internal")
    rwkv_d = nc.dram_tensor("rwkv_d", [D, S], BF16, kind="Internal")
    normed_d = nc.dram_tensor("normed_d", [D, S], BF16, kind="Internal")

    esg = contextlib.ExitStack()
    def mk_sb(stack):
        return lambda n, s_, d=F32: stack.enter_context(nc.sbuf_tensor(n, s_, d))
    sbg = mk_sb(esg)
    sem = lambda n: esg.enter_context(nc.semaphore(n))

    sv = sem("sv"); ss = sem("ss"); st_ = sem("st"); sg_ = sem("sg")
    NSL_DMA = 8
    d_sy = [sem(f"d_sy{i}") for i in range(NSL_DMA)]
    d_gp = [sem(f"d_gp{i}") for i in range(NSL_DMA)]
    d_sc = [sem(f"d_sc{i}") for i in range(NSL_DMA)]
    d_out = [sem(f"d_out{i}") for i in range(4)]
    V = Eng(nc, nc.vector, sv)
    SC = Eng(nc, nc.scalar, ss)
    T = Eng(nc, nc.tensor, st_)
    G = Eng(nc, nc.gpsimd, sg_)
    SY = Eng(nc, nc.sync, object(), dma_sems=d_sy)
    GQ = Eng(nc, nc.gpsimd, sg_, dma_sems=d_gp)
    SCQ = Eng(nc, nc.scalar, ss, dma_sems=d_sc)
    OQ = Eng(nc, nc.scalar, ss, dma_sems=d_out)
    GQ.watermark = G.watermark
    SCQ.watermark = SC.watermark
    OQ.watermark = SC.watermark
    pass

    TPOSE = object()
    vop = lambda b, r=(), w=(): TR.op(V, b, r, w)
    sop = lambda b, r=(), w=(): TR.op(SC, b, r, w)
    top = lambda b, r=(), w=(): TR.op(T, b, r, w)
    gop = lambda b, r=(), w=(): TR.op(G, b, r, w)
    dma_sy = lambda b, r=(), w=(): TR.op(SY, b, r, w, dma=True)
    dma_gp = lambda b, r=(), w=(): TR.op(GQ, b, r, w, dma=True)
    dma_sc = lambda b, r=(), w=(): TR.op(SCQ, b, r, w, dma=True)
    dma_out = lambda b, r=(), w=(): TR.op(OQ, b, r, w, dma=True)

    all_sems = [sv, ss, st_, sg_, d_sy, d_gp, d_out]

    def fence():
        vals = {id(sv): V.n, id(ss): SC.n, id(st_): T.n, id(sg_): G.n}
        for Q in (SY, GQ, SCQ, OQ):
            for s_, c_ in zip(Q.dma_sems, Q.dma_cnt):
                if c_:
                    nc.vector.wait_ge(s_, c_)
                    vals[id(s_)] = c_
        nc.all_engine_barrier()
        for E in (V, SC, T, G, SY):
            E.watermark.update(vals)

    ident = sbg("ident", [128, 128], BF16)
    selb = sbg("selb", [128, 2], BF16)
    selTb = sbg("selTb", [2, 128], BF16)
    tmr = sbg("tmr_s", [128, NCh]); tmk = sbg("tmk_s", [128, NCh]); tmv = sbg("tmv_s", [128, NCh])
    td = sbg("td_s", [128, NCh]); tf = sbg("tf_s", [128, NCh])
    lng = sbg("lng_s", [128, NCh]); lnb = sbg("lnb_s", [128, NCh])
    e1t = sbg("e1t", [128, NCh]); ewt = sbg("ewt", [128, NCh]); ewit = sbg("ewit", [128, NCh])
    eut = sbg("eut", [128, NCh]); eumw = sbg("eumw", [128, NCh])
    statS = sbg("statS", [128, 16]); statQ = sbg("statQ", [128, 16])
    scl128 = sbg("scl128", [128, NCh]); bia128 = sbg("bia128", [128, NCh])
    X0 = sbg("X0", [128, QW]); X1 = sbg("X1", [128, QW])
    X2 = sbg("X2", [128, QW]); X3 = sbg("X3", [128, QW])
    Qb1 = sbg("Qb1", [128, QW // 16]); Qb2 = sbg("Qb2", [128, QW // 16])
    Qout = sbg("Qout", [128, QW], BF16)
    Qout2 = sbg("Qout2", [128, QW], BF16)
    junkw = sbg("junkw", [128, QW])
    TR.op(V, lambda: nc.vector.memset(junkw[:, :], 0.0), (), (junkw,))

    def vguard(build_out_in, r=(), w=(), wd=64):
        return TR.op(V, build_out_in, r, w)
    vrecip = vguard


    PA = esg.enter_context(nc.psum_tensor("PA", [128, NHF], F32))
    PB = esg.enter_context(nc.psum_tensor("PB", [128, NHF], F32))
    PC = esg.enter_context(nc.psum_tensor("PC", [128, D], F32))

    for dst, src in ((ident, ident_d), (selb, sel_d), (selTb, selT_d)):
        dma_sy((lambda dst=dst, src=src: nc.sync.dma_start(dst[:, :], src.ap().bitcast(BF16))),
               w=[dst])
    for dst, src in ((tmr, tmr_h), (tmk, tmk_h), (tmv, tmv_h), (td, td_h),
                     (tf, tf_h), (lng, lng_h), (lnb, lnb_h)):
        dma_sy((lambda dst=dst, src=src: nc.sync.dma_start(dst[:, :], src.ap())), w=[dst])
    sop(lambda: nc.scalar.activation(e1t[:, :], td[:, :], AF.Exp), r=[td], w=[e1t])
    sop(lambda: nc.scalar.activation(ewt[:, :], e1t[:, :], AF.Exp, scale=-1.0), r=[e1t], w=[ewt])
    sop(lambda: nc.scalar.activation(ewit[:, :], e1t[:, :], AF.Exp, scale=1.0), r=[e1t], w=[ewit])
    sop(lambda: nc.scalar.activation(eut[:, :], tf[:, :], AF.Exp), r=[tf], w=[eut])
    vop(lambda: nc.vector.tensor_sub(eumw[:, :], eut[:, :], ewit[:, :]), r=[eut, ewit], w=[eumw])

    def expand16(dst, src, nblk):
        dw = dst.shape[1]
        for j in range(16):
            vop(lambda j=j: nc.vector.tensor_copy(
                   bass.AP(dst, j, [[dw, 128], [16, nblk]]), src[:, 0:nblk]),
               r=[src], w=[dst])

    def quant(src_ap, src_h, W_, out_ap, out_h):
        nblk = W_ // 16
        src3 = bass.AP(src_h, src_ap.offset, [[src_h.shape[1], 128], [16, nblk], [1, 16]])
        vop(lambda: nc.vector.tensor_reduce(Qb1[:, 0:nblk], src3, AX.X, OP.max,
                                            apply_absolute_value=True), r=[src_h], w=[Qb1])
        vop(lambda: nc.vector.tensor_scalar(Qb2[:, 0:nblk], Qb1[:, 0:nblk], 1.0 / 6.0, 1e-12,
                                            OP.mult, OP.max), r=[Qb1], w=[Qb2])
        vop(lambda: nc.vector.reciprocal(Qb1[:, 0:nblk], Qb2[:, 0:nblk]), r=[Qb2], w=[Qb1])
        expand16(X2, Qb1, nblk)
        vop(lambda: nc.vector.tensor_mul(X1[:, 0:W_], src_ap, X2[:, 0:W_]),
            r=[src_h, X2], w=[X1])
        vop(lambda: nc.vector.tensor_mul(X2[:, 0:W_], X1[:, 0:W_], X1[:, 0:W_]),
            r=[X1], w=[X2])
        MID2 = (0.0625, 0.5625, 1.5625, 3.0625, 6.25, 12.25, 25.0)
        DLT = (0.5, 0.5, 0.5, 0.5, 1.0, 1.0, 2.0)
        vop(lambda: nc.vector.tensor_scalar(X3[:, 0:W_], X2[:, 0:W_], MID2[0], DLT[0],
                                            OP.is_ge, OP.mult), r=[X2], w=[X3])
        for j in range(1, 7):
            vop(lambda j=j: nc.vector.tensor_scalar(junkw[:, 0:W_], X2[:, 0:W_], MID2[j],
                                                    DLT[j], OP.is_ge, OP.mult),
                r=[X2], w=[junkw])
            vop(lambda: nc.vector.tensor_add(X3[:, 0:W_], X3[:, 0:W_], junkw[:, 0:W_]),
                r=[X3, junkw], w=[X3])
        vop(lambda: nc.vector.tensor_scalar(X2[:, 0:W_], X1[:, 0:W_], 0.0, 2.0,
                                            OP.is_ge, OP.mult), r=[X1], w=[X2])
        vop(lambda: nc.vector.tensor_scalar_add(X2[:, 0:W_], X2[:, 0:W_], -1.0),
            r=[X2], w=[X2])
        vop(lambda: nc.vector.tensor_mul(X3[:, 0:W_], X3[:, 0:W_], X2[:, 0:W_]),
            r=[X3, X2], w=[X3])
        expand16(X2, Qb2, nblk)
        vop(lambda: nc.vector.tensor_mul(out_ap, X3[:, 0:W_], X2[:, 0:W_]),
            r=[X3, X2], w=[out_h])

    es1 = contextlib.ExitStack(); sb1 = mk_sb(es1)
    xa = sb1("xa", [128, D]); xb2 = sb1("xb2", [128, D])
    xnf = sb1("xnf", [128, D]); xnb = sb1("xnb", [128, D], BF16)
    lnScr = sb1("lnScr", [128, D])
    lnS = sb1("lnS", [128, 1]); lnQ = sb1("lnQ", [128, 1])
    lnT1 = sb1("lnT1", [128, 1]); lnT2 = sb1("lnT2", [128, 1])
    lnR = sb1("lnR", [128, 1]); lnB = sb1("lnB", [128, 1])
    wnat = [sb1("wnatA", [128, NF]), sb1("wnatB", [128, NF])]
    tstage = sb1("tstage", [128, S], BF16)

    xtiles = [xa, xb2]
    for i in range(NT):
        xt_ = xtiles[i % 2]
        dma_sy(lambda i=i, xt_=xt_: nc.sync.dma_start(xt_[:, :],
                                                      x_h.ap()[128 * i:128 * (i + 1), :]),
               w=[xt_])
        vguard(lambda xt_=xt_: nc.vector.reduce_sum(lnS[:, :], xt_[:, :], AX.X),
            r=[xt_], w=[lnS], wd=D)
        sop(lambda xt_=xt_: nc.scalar.activation(X3[:, 0:D], xt_[:, :], AF.Square), r=[xt_], w=[X3])
        vguard(lambda: nc.vector.reduce_sum(lnQ[:, :], X3[:, 0:D], AX.X), r=[X3], w=[lnQ], wd=D)
        vop(lambda: nc.vector.tensor_scalar_mul(lnT1[:, :], lnS[:, :], 1.0 / D), r=[lnS], w=[lnT1])
        vop(lambda: nc.vector.tensor_scalar_mul(lnT2[:, :], lnQ[:, :], 1.0 / D), r=[lnQ], w=[lnT2])
        vop(lambda: nc.vector.tensor_mul(lnS[:, :], lnT1[:, :], lnT1[:, :]), r=[lnT1], w=[lnS])
        vop(lambda: nc.vector.tensor_sub(lnT2[:, :], lnT2[:, :], lnS[:, :]), r=[lnT2, lnS], w=[lnT2])
        vop(lambda: nc.vector.tensor_scalar_add(lnT2[:, :], lnT2[:, :], 1e-5), r=[lnT2], w=[lnT2])
        sop(lambda: nc.scalar.activation(lnT2[:, :], lnT2[:, :], AF.Ln), r=[lnT2], w=[lnT2])
        sop(lambda: nc.scalar.activation(lnR[:, :], lnT2[:, :], AF.Exp, scale=-0.5),
            r=[lnT2], w=[lnR])
        vop(lambda: nc.vector.scalar_tensor_tensor(lnB[:, :], lnT1[:, :], -1.0, lnR[:, :],
                                                   OP.mult, OP.mult), r=[lnT1, lnR], w=[lnB])
        vop(lambda xt_=xt_: nc.vector.tensor_scalar(xnf[:, :], xt_[:, :], lnR[:, :],
                                                    lnB[:, :], OP.mult, OP.add),
            r=[xt_, lnR, lnB], w=[xnf])
        sop(lambda: nc.scalar.copy(xnb[:, :], xnf[:, :]), r=[xnf], w=[xnb])
        dma_sc(lambda i=i: nc.scalar.dma_start(xnbf_d.ap()[128 * i:128 * (i + 1), :],
                                               xnb[:, :]), r=[xnb], w=[xnbf_d])
        quant(xnf[:, 0:D], xnf, D, Qout[:, 0:D], Qout)
        dma_sc(lambda i=i: nc.scalar.dma_start(xq_d.ap()[128 * i:128 * (i + 1), :],
                                               Qout[:, 0:D]), r=[Qout], w=[xq_d])

    fence()
    for j in range(NCh):
        dma_sy(lambda j=j: nc.sync.dma_start(tstage[:, :], xq_d.ap()[:, 128 * j:128 * (j + 1)],
                                             transpose=True), r=[xq_d, TPOSE], w=[tstage, TPOSE])
        dma_sc(lambda j=j: nc.scalar.dma_start(xqT_d.ap()[:, S * j:S * (j + 1)], tstage[:, :]),
               r=[tstage], w=[xqT_d])
    fence()
    fence()
    for w_h, wq_dr, nrows, rowW in ((wg_h, wgq_d, NF, D), (wu_h, wuq_d, NF, D),
                                    (wd_h, wdq_d, D, NF)):
        for c in range(nrows // 128):
            wn = wnat[c % 2]
            dma_sy(lambda w_h=w_h, c=c, wn=wn, rowW=rowW:
                   nc.sync.dma_start(wn[:, 0:rowW], w_h.ap()[128 * c:128 * (c + 1), :]), w=[wn])
            nh = 1 if rowW <= QW else 2
            for hh in range(nh):
                w0 = (rowW // nh) * hh
                ww = rowW // nh
                quant(wn[:, w0:w0 + ww], wn, ww, Qout[:, 0:ww], Qout)
                dma_sc(lambda wq_dr=wq_dr, c=c, w0=w0, ww=ww:
                       nc.scalar.dma_start(wq_dr.ap()[128 * c:128 * (c + 1), w0:w0 + ww],
                                           Qout[:, 0:ww]), r=[Qout], w=[wq_dr])

    es1.close()

    es23 = contextlib.ExitStack(); sb23 = mk_sb(es23)
    xrT = [sb23(f"xrT{j}", [128, S], BF16) for j in range(NCh)]
    xkT = [sb23(f"xkT{j}", [128, S], BF16) for j in range(NCh)]
    xvT = [sb23(f"xvT{j}", [128, S], BF16) for j in range(NCh)]
    es2 = contextlib.ExitStack(); sb2 = mk_sb(es2)
    OFF = 16
    xnT = [sb2(f"xnT{j}", [128, S + OFF], BF16) for j in range(NCh)]
    dmix = sb2("dmix", [128, S], BF16)
    for j in range(NCh):
        xj = xnT[j]
        vop(lambda xj=xj: nc.vector.memset(xj[:, OFF - 1:OFF], 0.0), w=[xj])
        dma_sy(lambda j=j, xj=xj: nc.sync.dma_start(
                   xj[:, OFF:S + OFF], xnbf_d.ap()[:, 128 * j:128 * (j + 1)], transpose=True),
               r=[xnbf_d, TPOSE], w=[xj, TPOSE])
        vop(lambda xj=xj: nc.vector.tensor_sub(dmix[:, :], xj[:, OFF:S + OFF],
                                               xj[:, OFF - 1:S + OFF - 1]), r=[xj], w=[dmix])
        for lam, dst in ((tmr, xrT[j]), (tmk, xkT[j]), (tmv, xvT[j])):
            vop(lambda lam=lam, dst=dst, xj=xj, j=j:
                nc.vector.scalar_tensor_tensor(dst[:, :], dmix[:, :], lam[:, j:j + 1],
                                               xj[:, OFF - 1:S + OFF - 1], OP.mult, OP.add),
                r=[dmix, lam, xj], w=[dst])
    fence()
    es2.close()

    es3 = contextlib.ExitStack(); sb3 = mk_sb(es3)
    wsl = {w: [sb3(f"wsl_{w}{b}", [128, D], BF16) for b in range(2)] for w in "rkv"}
    wslf = {w: [sb3(f"wslf_{w}{b}", [128, D]) for b in range(2)] for w in "rkv"}
    wsrc = {"r": wrT_h, "k": wkT_h, "v": wvT_h}
    sgr = sb3("sgr", [128, HALF])
    f_t = sb3("f_t", [128, HALF])
    ewfull = sb3("ewfull", [128, HALF])
    rwkvf = sb3("rwkvf", [128, HALF])
    rwkvb = sb3("rwkvb", [128, HALF], BF16)
    B0t = sb3("B0t", [128, HALF + 1]); Bt = sb3("Bt", [128, HALF + 1])
    gt = sb3("gt", [128, HALF + 1]); At = sb3("At", [128, HALF + 1])
    cB0 = sb3("cB0", [128, 1]); cB = sb3("cB", [128, 1]); cA = sb3("cA", [128, 1])
    ekt = lambda: X0[:, 0:HALF]
    ekvt = lambda: X1[:, 0:HALF]
    sc1 = lambda: X2[:, 0:HALF]
    sc2 = lambda: X3[:, 0:HALF]
    r_ps = lambda: PA.ap()[:, 0:HALF]
    k_ps = lambda: PB.ap()[:, 0:HALF]
    v_ps = lambda: PC.ap()[:, 0:HALF]

    for c in range(NCh):
        vop(lambda: nc.vector.memset(ewfull[:, :], 1.0), w=[ewfull])
        vop(lambda c=c: nc.vector.tensor_scalar_mul(ewfull[:, :], ewfull[:, :], ewt[:, c:c + 1]),
            r=[ewfull, ewt], w=[ewfull])
        for wname in "rkv":
            wb = wsl[wname][c % 2]
            wf_ = wslf[wname][c % 2]
            dma_sy(lambda wname=wname, c=c, wf_=wf_: nc.sync.dma_start(
                       wf_.ap().rearrange("p (f q) -> p f q", f=NCh),
                       wsrc[wname].ap().rearrange("(f p) q -> p f q", p=128)
                       [:, :, 128 * c:128 * (c + 1)]), w=[wf_])
            sop(lambda wb=wb, wf_=wf_: nc.scalar.copy(wb[:, :], wf_[:, :]), r=[wf_], w=[wb])
        for t_ in (cB0, cB, cA):
            vop(lambda t_=t_: nc.vector.memset(t_[:, :], 0.0), w=[t_])
        vop(lambda: nc.vector.memset(B0t[:, 0:1], 0.0), w=[B0t])
        vop(lambda: nc.vector.memset(Bt[:, 0:1], 0.0), w=[Bt])
        vop(lambda: nc.vector.memset(gt[:, 0:1], 1.0), w=[gt])

        for th in range(2):
            t0 = th * HALF
            for wname, pst, psum_t, xsrc in (("r", r_ps, PA, xrT), ("k", k_ps, PB, xkT),
                                             ("v", v_ps, PC, xvT)):
                wb = wsl[wname][c % 2]
                for fj in range(NCh):
                    for nsl in range(2):
                        top(lambda wb=wb, fj=fj, nsl=nsl, pst=pst, xsrc=xsrc, t0=t0:
                            nc.tensor.matmul(pst()[:, 512 * nsl:512 * (nsl + 1)],
                                             wb.ap().rearrange("p (f q) -> p f q", f=NCh)[:, fj, :],
                                             xsrc[fj][:, t0 + 512 * nsl:t0 + 512 * (nsl + 1)],
                                             start=(fj == 0), stop=(fj == NCh - 1)),
                            r=[wb, xsrc[fj]], w=[psum_t])
            sop(lambda: nc.scalar.activation(ekt(), k_ps(), AF.Exp), r=[PB], w=[X0])
            vop(lambda: nc.vector.tensor_mul(ekvt(), ekt(), v_ps()), r=[X0, PC], w=[X1])
            sop(lambda: nc.scalar.activation(sgr[:, :], r_ps(), AF.Exp, scale=-1.0),
                r=[PA], w=[sgr])
            vop(lambda: nc.vector.tensor_scalar_add(sgr[:, :], sgr[:, :], 1.0), r=[sgr], w=[sgr])
            vrecip(lambda: nc.vector.reciprocal(sgr[:, :], sgr[:, :]), r=[sgr], w=[sgr], wd=HALF)
            vop(lambda: nc.vector.tensor_tensor_scan(B0t[:, 1:], ewfull[:, :], ekt(),
                                                     cB0[:, :], OP.mult, OP.add),
                r=[ewfull, X0, cB0], w=[B0t])
            vop(lambda: nc.vector.tensor_copy(cB0[:, :], B0t[:, HALF:]), r=[B0t], w=[cB0])
            vop(lambda c=c: nc.vector.tensor_scalar_mul(sc1(), B0t[:, 0:HALF], ewt[:, c:c + 1]),
                r=[B0t, ewt], w=[X2])
            vop(lambda: nc.vector.tensor_max(sc1(), sc1(), ekt()), r=[X2, X0], w=[X2])
            vrecip(lambda: nc.vector.reciprocal(sc2(), B0t[:, 1:]), r=[B0t], w=[X3], wd=HALF)
            vop(lambda: nc.vector.tensor_mul(sc1(), sc1(), sc2()), r=[X2, X3], w=[X2])
            vop(lambda: nc.vector.tensor_scalar(f_t[:, :], sc1(), EPS, 1.0, OP.mult, OP.add),
                r=[X2], w=[f_t])
            vop(lambda c=c: nc.vector.tensor_scalar_mul(sc1(), f_t[:, :], ewt[:, c:c + 1]),
                r=[f_t, ewt], w=[X2])
            vop(lambda: nc.vector.tensor_mul(sc2(), f_t[:, :], ekt()), r=[f_t, X0], w=[X3])
            vop(lambda: nc.vector.tensor_tensor_scan(Bt[:, 1:], sc1(), sc2(), cB[:, :],
                                                     OP.mult, OP.add), r=[X2, X3, cB], w=[Bt])
            vop(lambda: nc.vector.tensor_copy(cB[:, :], Bt[:, HALF:]), r=[Bt], w=[cB])
            vop(lambda c=c: nc.vector.tensor_scalar_mul(sc1(), Bt[:, 0:HALF], ewt[:, c:c + 1]),
                r=[Bt, ewt], w=[X2])
            vop(lambda: nc.vector.tensor_max(sc1(), sc1(), ekt()), r=[X2, X0], w=[X2])
            vrecip(lambda: nc.vector.reciprocal(sc1(), sc1()), r=[X2], w=[X2], wd=HALF)
            vop(lambda: nc.vector.tensor_mul(gt[:, 1:], Bt[:, 1:], sc1()), r=[Bt, X2], w=[gt])
            vop(lambda c=c: nc.vector.tensor_scalar_mul(sc1(), gt[:, 0:HALF], ewt[:, c:c + 1]),
                r=[gt, ewt], w=[X2])
            vop(lambda: nc.vector.tensor_tensor_scan(At[:, 1:], sc1(), ekvt(), cA[:, :],
                                                     OP.mult, OP.add), r=[X2, X1, cA], w=[At])
            vop(lambda: nc.vector.tensor_copy(cA[:, :], At[:, HALF:]), r=[At], w=[cA])
            vop(lambda: nc.vector.tensor_copy(gt[:, 0:1], gt[:, HALF:]), r=[gt], w=[gt])
            vop(lambda c=c: nc.vector.tensor_scalar_mul(sc1(), At[:, 1:], ewit[:, c:c + 1]),
                r=[At, ewit], w=[X2])
            vop(lambda c=c: nc.vector.scalar_tensor_tensor(sc1(), ekvt(), eumw[:, c:c + 1],
                                                           sc1(), OP.mult, OP.add),
                r=[X1, eumw, X2], w=[X2])
            vop(lambda c=c: nc.vector.tensor_scalar_mul(ekt(), ekt(), eut[:, c:c + 1]),
                r=[X0, eut], w=[X0])
            vop(lambda: nc.vector.tensor_max(sc2(), Bt[:, 0:HALF], ekt()), r=[Bt, X0], w=[X3])
            vop(lambda: nc.vector.tensor_add(ekt(), Bt[:, 0:HALF], ekt()), r=[Bt, X0], w=[X0])
            vop(lambda: nc.vector.scalar_tensor_tensor(sc2(), sc2(), EPS, ekt(),
                                                       OP.mult, OP.add), r=[X3, X0], w=[X3])
            vrecip(lambda: nc.vector.reciprocal(sc2(), sc2()), r=[X3], w=[X3], wd=HALF)
            vop(lambda: nc.vector.tensor_mul(sc1(), sc1(), sc2()), r=[X2, X3], w=[X2])
            col = 2 * c + th
            vop(lambda: nc.vector.tensor_mul(rwkvf[:, :], sc1(), sgr[:, :]),
                r=[X2, sgr], w=[rwkvf])
            vguard(lambda col=col: nc.vector.reduce_sum(statS[:, col:col + 1], rwkvf[:, :], AX.X),
                r=[rwkvf], w=[statS], wd=HALF)
            sop(lambda: nc.scalar.activation(ekvt(), rwkvf[:, :], AF.Square), r=[rwkvf], w=[X1])
            vguard(lambda col=col: nc.vector.reduce_sum(statQ[:, col:col + 1], ekvt(), AX.X),
                r=[X1], w=[statQ], wd=HALF)
            sop(lambda: nc.scalar.copy(rwkvb[:, :], rwkvf[:, :]), r=[rwkvf], w=[rwkvb])
            dma_sc(lambda c=c, t0=t0: nc.scalar.dma_start(
                       rwkv_d.ap()[128 * c:128 * (c + 1), t0:t0 + HALF], rwkvb[:, :]),
                   r=[rwkvb], w=[rwkv_d])
            vop(lambda: nc.vector.tensor_copy(B0t[:, 0:1], cB0[:, :]), r=[cB0], w=[B0t])
            vop(lambda: nc.vector.tensor_copy(Bt[:, 0:1], cB[:, :]), r=[cB], w=[Bt])
    fence()
    es3.close()
    es23.close()

    es4 = contextlib.ExitStack(); sb4 = mk_sb(es4)
    gs8 = sb4("gs8", [128, NCh]); gq8 = sb4("gq8", [128, NCh])
    hi_b = sb4("hi_b", [128, 2 * NCh], BF16); lo_b = sb4("lo_b", [128, 2 * NCh], BF16)
    lo_f = sb4("lo_f", [128, 2 * NCh])
    tot2 = sb4("tot2", [2, 2 * NCh]); tot2b = sb4("tot2b", [2, 2 * NCh], BF16)
    sall = sb4("sall", [128, NCh]); qall = sb4("qall", [128, NCh])
    gtmp1 = sb4("gtmp1", [128, NCh]); gtmp2 = sb4("gtmp2", [128, NCh])
    rload = sb4("rload", [128, S], BF16)
    nstage = sb4("nstage", [128, S], BF16)

    s_str = lambda tns, off: bass.AP(tns, off, [[16, 128], [2, NCh]])
    vop(lambda: nc.vector.tensor_add(gs8[:, :], s_str(statS, 0), s_str(statS, 1)),
        r=[statS], w=[gs8])
    vop(lambda: nc.vector.tensor_add(gq8[:, :], s_str(statQ, 0), s_str(statQ, 1)),
        r=[statQ], w=[gq8])
    vop(lambda: nc.vector.tensor_copy(hi_b[:, 0:NCh], gs8[:, :]), r=[gs8], w=[hi_b])
    vop(lambda: nc.vector.tensor_copy(hi_b[:, NCh:], gq8[:, :]), r=[gq8], w=[hi_b])
    vop(lambda: nc.vector.tensor_sub(lo_f[:, 0:NCh], gs8[:, :], hi_b[:, 0:NCh]),
        r=[gs8, hi_b], w=[lo_f])
    vop(lambda: nc.vector.tensor_sub(lo_f[:, NCh:], gq8[:, :], hi_b[:, NCh:]),
        r=[gq8, hi_b], w=[lo_f])
    vop(lambda: nc.vector.tensor_copy(lo_b[:, :], lo_f[:, :]), r=[lo_f], w=[lo_b])
    top(lambda: nc.tensor.matmul(PC.ap()[0:2, 0:2 * NCh], selb[:, :], hi_b[:, :],
                                 start=True, stop=False), r=[selb, hi_b], w=[PC])
    top(lambda: nc.tensor.matmul(PC.ap()[0:2, 0:2 * NCh], selb[:, :], lo_b[:, :],
                                 start=False, stop=True), r=[selb, lo_b], w=[PC])
    vop(lambda: nc.vector.tensor_copy(tot2[:, :], PC.ap()[0:2, 0:2 * NCh]), r=[PC], w=[tot2])
    vop(lambda: nc.vector.tensor_copy(tot2b[:, :], tot2[:, :]), r=[tot2], w=[tot2b])
    top(lambda: nc.tensor.matmul(PC.ap()[:, 0:2 * NCh], selTb[:, :], tot2b[:, :],
                                 start=True, stop=True), r=[selTb, tot2b], w=[PC])
    vop(lambda: nc.vector.tensor_copy(sall[:, :], PC.ap()[:, 0:NCh]), r=[PC], w=[sall])
    vop(lambda: nc.vector.tensor_copy(qall[:, :], PC.ap()[:, NCh:2 * NCh]), r=[PC], w=[qall])
    NEL = float(S * HD)
    vop(lambda: nc.vector.tensor_scalar_mul(sall[:, :], sall[:, :], 1.0 / NEL), r=[sall], w=[sall])
    vop(lambda: nc.vector.tensor_scalar_mul(qall[:, :], qall[:, :], 1.0 / NEL), r=[qall], w=[qall])
    vop(lambda: nc.vector.tensor_mul(gtmp1[:, :], sall[:, :], sall[:, :]), r=[sall], w=[gtmp1])
    vop(lambda: nc.vector.tensor_sub(qall[:, :], qall[:, :], gtmp1[:, :]),
        r=[qall, gtmp1], w=[qall])
    vop(lambda: nc.vector.tensor_scalar_add(qall[:, :], qall[:, :], 1e-5), r=[qall], w=[qall])
    sop(lambda: nc.scalar.activation(qall[:, :], qall[:, :], AF.Ln), r=[qall], w=[qall])
    sop(lambda: nc.scalar.activation(gtmp2[:, :], qall[:, :], AF.Exp, scale=-0.5),
        r=[qall], w=[gtmp2])
    vop(lambda: nc.vector.tensor_mul(scl128[:, :], gtmp2[:, :], lng[:, :]),
        r=[gtmp2, lng], w=[scl128])
    vop(lambda: nc.vector.tensor_mul(gtmp1[:, :], sall[:, :], scl128[:, :]),
        r=[sall, scl128], w=[gtmp1])
    vop(lambda: nc.vector.tensor_sub(bia128[:, :], lnb[:, :], gtmp1[:, :]),
        r=[lnb, gtmp1], w=[bia128])
    for c in range(NCh):
        dma_sy(lambda c=c: nc.sync.dma_start(rload[:, :], rwkv_d.ap()[128 * c:128 * (c + 1), :]),
               r=[rwkv_d], w=[rload])
        vop(lambda c=c: nc.vector.tensor_scalar(nstage[:, :], rload[:, :], scl128[:, c:c + 1],
                                                bia128[:, c:c + 1], OP.mult, OP.add),
            r=[rload, scl128, bia128], w=[nstage])
        dma_sc(lambda c=c: nc.scalar.dma_start(normed_d.ap()[128 * c:128 * (c + 1), :],
                                               nstage[:, :]), r=[nstage], w=[normed_d])
    fence()
    es4.close()

    es5 = contextlib.ExitStack(); sb5 = mk_sb(es5)
    wgT = [sb5(f"wgT{j}", [128, NF], BF16) for j in range(NCh)]
    wuT = [sb5(f"wuT{j}", [128, NF], BF16) for j in range(NCh)]
    woT = [sb5(f"woT{j}", [128, D], BF16) for j in range(NCh)]
    wdT_st = sb5("wdT_st", [128, D], BF16)
    wdq_sl = sb5("wdq_sl", [128, (NNC // 2) * D], BF16)
    xqT_sl = [sb5("xqT_A", [128, NCh, 128], BF16), sb5("xqT_B", [128, NCh, 128], BF16)]
    nrm_sl = [sb5("nrm_A", [128, NCh, 128], BF16), sb5("nrm_B", [128, NCh, 128], BF16)]
    x_sl = [sb5("x_slA", [128, D]), sb5("x_slB", [128, D])]

    for j in range(NCh):
        dma_sy(lambda j=j: nc.sync.dma_start(wgT[j][:, :], wgq_d.ap()[:, 128 * j:128 * (j + 1)],
                                             transpose=True), r=[wgq_d, TPOSE], w=[wgT[j], TPOSE])
        dma_sy(lambda j=j: nc.sync.dma_start(wuT[j][:, :], wuq_d.ap()[:, 128 * j:128 * (j + 1)],
                                             transpose=True), r=[wuq_d, TPOSE], w=[wuT[j], TPOSE])
    for nj in range(NNC):
        dma_sy(lambda nj=nj: nc.sync.dma_start(wdT_st[:, :],
                                               wdq_d.ap()[:, 128 * nj:128 * (nj + 1)],
                                               transpose=True), r=[wdq_d, TPOSE], w=[wdT_st, TPOSE])
        dma_sc(lambda nj=nj: nc.scalar.dma_start(wdqT_d.ap()[:, D * nj:D * (nj + 1)],
                                                 wdT_st[:, :]), r=[wdT_st], w=[wdqT_d])
    fence()
    woTf = sb5("woTf", [128, D])
    for j in range(NCh):
        dma_sy(lambda j=j: nc.sync.dma_start(woTf[:, :],
                                             woT_h.ap()[128 * j:128 * (j + 1), :]), w=[woTf])
        sop(lambda j=j: nc.scalar.copy(woT[j][:, :], woTf[:, :]), r=[woTf], w=[woT[j]])

    gate_ps = lambda: PA.ap()[:, 0:NHF]
    up_ps = lambda: PB.ap()[:, 0:NHF]
    down_ps = lambda: PC.ap()
    hqT_ps = lambda: PA.ap().bitcast(BF16)[:, 0:NHF]
    s_ps = lambda: PB.ap()[:, 0:D]
    NSL = ((0, 512), (512, 1024), (1024, NHF))

    for i in range(NT):
        t0 = 128 * i
        xq_b = xqT_sl[i % 2]
        nr_b = nrm_sl[i % 2]
        xx = x_sl[i % 2]
        dma_sy(lambda i=i, xq_b=xq_b: nc.sync.dma_start(
                   xq_b[:, :, :],
                   xqT_d.ap().rearrange("p (a t) -> p a t", a=NCh)[:, :, t0:t0 + 128]),
               r=[xqT_d], w=[xq_b])
        dma_sy(lambda i=i, nr_b=nr_b: nc.sync.dma_start(
                   nr_b[:, :, :],
                   normed_d.ap().rearrange("(a p) t -> p a t", p=128)[:, :, t0:t0 + 128]),
               r=[normed_d], w=[nr_b])
        dma_sy(lambda i=i, xx=xx: nc.sync.dma_start(xx[:, :], x_h.ap()[t0:t0 + 128, :]),
               w=[xx])
        for hh in range(2):
            n0 = NHF * hh
            dma_sy(lambda hh=hh: nc.sync.dma_start(
                       wdq_sl[:, :], wdqT_d.ap()[:, (NNC // 2) * D * hh:(NNC // 2) * D * (hh + 1)]),
                   r=[wdqT_d], w=[wdq_sl])
            for fj in range(NCh):
                for (a, b) in NSL:
                    top(lambda fj=fj, a=a, b=b, xq_b=xq_b, n0=n0:
                        nc.tensor.matmul(gate_ps()[:, a:b], xq_b[:, fj, :],
                                         wgT[fj][:, n0 + a:n0 + b],
                                         start=(fj == 0), stop=(fj == NCh - 1)),
                        r=[xq_b, wgT[fj]], w=[PA])
                for (a, b) in NSL:
                    top(lambda fj=fj, a=a, b=b, xq_b=xq_b, n0=n0:
                        nc.tensor.matmul(up_ps()[:, a:b], xq_b[:, fj, :],
                                         wuT[fj][:, n0 + a:n0 + b],
                                         start=(fj == 0), stop=(fj == NCh - 1)),
                        r=[xq_b, wuT[fj]], w=[PB])
            sop(lambda: nc.scalar.activation(X2[:, 0:NHF], gate_ps(), AF.Silu), r=[PA], w=[X2])
            vop(lambda: nc.vector.tensor_mul(X0[:, 0:NHF], X2[:, 0:NHF], up_ps()),
                r=[X2, PB], w=[X0])
            quant(X0[:, 0:NHF], X0, NHF, Qout[:, 0:NHF], Qout)
            for pj in range(NHF // 128):
                top(lambda pj=pj: nc.tensor.transpose(
                        hqT_ps()[:, 128 * pj:128 * (pj + 1)], Qout[:, 128 * pj:128 * (pj + 1)],
                        ident[:, :]), r=[Qout, ident], w=[PA])
            sop(lambda: nc.scalar.copy(Qout2[:, 0:NHF], hqT_ps()), r=[PA], w=[Qout2])
            for pj in range(NHF // 128):
                nj = hh * (NHF // 128) + pj
                for (a, b) in ((0, 512), (512, 1024)):
                    top(lambda pj=pj, nj=nj, a=a, b=b:
                        nc.tensor.matmul(down_ps()[:, a:b], Qout2[:, 128 * pj:128 * (pj + 1)],
                                         wdq_sl.ap().rearrange("p (n q) -> p n q", n=NNC // 2)
                                         [:, pj, a:b],
                                         start=(nj == 0), stop=(nj == NNC - 1)),
                        r=[Qout2, wdq_sl], w=[PC])
        sop(lambda: nc.scalar.activation(X1[:, 0:D], down_ps(), AF.Silu), r=[PC], w=[X1])
        for cj in range(NCh):
            for (a, b) in ((0, 512), (512, 1024)):
                top(lambda cj=cj, a=a, b=b, nr_b=nr_b:
                    nc.tensor.matmul(s_ps()[:, a:b], nr_b[:, cj, :], woT[cj][:, a:b],
                                     start=(cj == 0), stop=(cj == NCh - 1)),
                    r=[nr_b, woT[cj]], w=[PB])
        vop(lambda: nc.vector.tensor_mul(X3[:, 0:D], X1[:, 0:D], s_ps()), r=[X1, PB], w=[X3])
        vop(lambda xx=xx: nc.vector.tensor_add(X3[:, 0:D], X3[:, 0:D], xx[:, :]),
            r=[X3, xx], w=[X3])
        dma_out(lambda i=i: nc.scalar.dma_start(out_h.ap()[t0:t0 + 128, :], X3[:, 0:D]),
                r=[X3])

    for Q in (SCQ, OQ):
        for s_, c_ in zip(Q.dma_sems, Q.dma_cnt):
            if c_:
                nc.sync.wait_ge(s_, c_)
    es5.close()
    esg.close()
    return nc


_cached = {}

def kernel(L0_attn_x, L0_attn_time_mix_r, L0_attn_time_mix_k, L0_attn_time_mix_v,
           L0_attn_W_r, L0_attn_W_k, L0_attn_W_v, L0_attn_W_o,
           L0_attn_time_decay, L0_attn_time_first, L0_attn_ln_gamma, L0_attn_ln_beta,
           L0_ffn_w_gate, L0_ffn_w_up, L0_ffn_w_down):
    x = np.ascontiguousarray(np.asarray(L0_attn_x, dtype=np.float32))
    B = x.shape[0]
    f32c = lambda a: np.ascontiguousarray(np.asarray(a, dtype=np.float32))
    tT = lambda a: np.ascontiguousarray(np.asarray(a, dtype=np.float32).T)
    pp = lambda a: np.ascontiguousarray(
        np.asarray(a, dtype=np.float32).reshape(-1).reshape(NCh, 128).T)

    shared = {
        "wrT": tT(L0_attn_W_r), "wkT": tT(L0_attn_W_k), "wvT": tT(L0_attn_W_v),
        "woT": tT(L0_attn_W_o),
        "wg": f32c(L0_ffn_w_gate), "wu": f32c(L0_ffn_w_up), "wd": f32c(L0_ffn_w_down),
        "tmr": pp(L0_attn_time_mix_r), "tmk": pp(L0_attn_time_mix_k),
        "tmv": pp(L0_attn_time_mix_v),
        "td": pp(L0_attn_time_decay), "tf": pp(L0_attn_time_first),
        "lng": pp(L0_attn_ln_gamma), "lnb": pp(L0_attn_ln_beta),
    }
    if "nc" not in _cached:
        _cached["nc"] = build_nc()
    nc = _cached["nc"]
    in_maps = [dict(shared, x=np.ascontiguousarray(x[b])) for b in range(B)]
    res = run_bass_kernel_spmd(nc, in_maps, core_ids=list(range(B)))
    return np.stack([r["out"] for r in res.results]).astype(np.float32)


